# version 14
# baseline (speedup 1.0000x reference)
import os
from collections import defaultdict
from contextlib import ExitStack

import numpy as np
import ml_dtypes

import concourse.bass as bass
import concourse.mybir as mybir
from concourse.bass_utils import run_bass_kernel_spmd

F32 = mybir.dt.float32
BF16 = mybir.dt.bfloat16
AF = mybir.ActivationFunctionType
ALU = mybir.AluOpType
AX = mybir.AxisListType
BFNP = ml_dtypes.bfloat16

S = 2048
D = 4096
NT = S // 128
NCH = D // 128
H = 16
HD = 256
ROT = 64
EPS = 1e-5
MASKNEG = -3.0e38

ENGINES = ("sync", "scalar", "vector", "tensor", "gpsimd")

DEBUG = os.environ.get("MK_DEBUG", "")
PHASES = os.environ.get("MK_PHASES", "A,B1,B2,B3,C,C2,D").split(",")
NHEADS = int(os.environ.get("MK_NHEADS", str(H)))
MK_NT = int(os.environ.get("MK_NT", str(NT)))
A_LVL = int(os.environ.get("MK_ALVL", "9"))


class Res:

    def __init__(self):
        self.w = []
        self.r = []

    def write(self, tok):
        self.w = [tok]
        self.r = []

    def read(self, tok):
        self.r.append(tok)

    def for_read(self):
        return list(self.w)

    def for_write(self):
        return list(self.w) + list(self.r)


class Sched:
    def __init__(self, nc):
        self.nc = nc
        self.q = {e: [] for e in ENGINES}
        self.cnt = defaultdict(int)
        self.sems = {}
        self.pend = {e: [] for e in ENGINES}
        self.active = {e: False for e in ENGINES}

    def sem(self, name):
        if name not in self.sems:
            self.sems[name] = self.nc.alloc_semaphore(name)
        return self.sems[name]

    def do(self, eng, f, waits=(), inc=None, dma=False):
        wd = {}
        for w_ in waits:
            assert w_ is not None, "None token in waits (producer missing inc)"
        for s, v in waits:
            if v > 0:
                wd[s] = max(wd.get(s, 0), v)
        tok = None
        if inc is not None:
            k = 16 if dma else 1
            self.cnt[inc] += k
            tok = (inc, self.cnt[inc])
        rec = {"f": f, "waits": list(self.pend[eng]) + sorted(wd.items()),
               "inc": (inc, 16 if dma else 1) if inc else None}
        self.pend[eng] = []
        self.q[eng].append(rec)
        self.active[eng] = True
        return tok

    def barrier(self):
        for e in ("scalar", "vector", "tensor"):
            if self.active[e]:
                sname = "prog_" + e
                self.cnt[sname] += 1
                v = self.cnt[sname]
                self.q[e].append({"f": None, "waits": [], "inc": None,
                                  "semset": (sname, v)})
        targets = [(s, v) for s, v in self.cnt.items() if v > 0]
        for e in ENGINES:
            self.pend[e] = list(targets)
            self.active[e] = False

    def flush(self, name):
        nc = self.nc
        if not any(self.q[e] for e in ENGINES):
            return
        with nc.Block(name) as blk:
            for e in ENGINES:
                ops = self.q[e]
                if not ops:
                    continue

                def body(engobj, ops=ops):
                    for rec in ops:
                        for s, v in rec["waits"]:
                            engobj.wait_ge(self.sem(s), v)
                        if rec.get("semset"):
                            s, v = rec["semset"]
                            engobj.sem_inc(self.sem(s), 1)
                            continue
                        ins = rec["f"](engobj)
                        if rec["inc"]:
                            s, k = rec["inc"]
                            ins.then_inc(self.sem(s), k)

                getattr(blk, e)(body)
        self.q = {e: [] for e in ENGINES}


def _host_consts():
    inv_freq = (
        1.0
        / np.power(
            np.float32(10000.0),
            np.arange(0, ROT, 2, dtype=np.float32) / np.float32(ROT),
        )
    ).astype(np.float32)
    t = np.arange(S, dtype=np.float32)
    sinu = np.einsum("i,j->ij", t, inv_freq).astype(np.float32)
    sin = np.sin(sinu).astype(np.float32)
    cos = np.cos(sinu).astype(np.float32)
    cos_f = np.repeat(cos.T, 2, axis=0)
    sin_f = np.repeat(sin.T, 2, axis=0)
    cs_q = np.concatenate([cos_f / 16.0, sin_f / 16.0], axis=0).astype(np.float32)
    cs_k = np.concatenate([cos_f, sin_f], axis=0).astype(np.float32)
    tri = np.where(
        np.arange(128)[None, :] <= np.arange(128)[:, None], 0.0, MASKNEG
    ).astype(np.float32)
    idn = np.eye(128, dtype=np.float32)
    e96 = np.zeros((3, 96), dtype=np.float32)
    for n in range(3):
        e96[n, n * 32: n * 32 + 16] = 1.0
    c3 = np.array([[2.0], [3.0], [4.0]], dtype=np.float32)
    return {"cs_q": cs_q, "cs_k": cs_k, "tri": tri, "idn": idn,
            "e96": e96, "c3": c3}


def _ln_tile(sc, nm, x_ap, bn6, mv, rstd, eps_sb, deps, res_x, res_stats,
             lvl=9):
    w_x = res_x.for_read() + deps
    w_s = res_stats.for_write()
    tk = None
    for u in range(8):
        tk = sc.do("vector", lambda e, u=u: e.bn_stats(
            bn6[:, u * 6:(u + 1) * 6], x_ap[:, u * 512:(u + 1) * 512]),
            waits=(w_x + w_s) if u == 0 else [], inc=nm + "_dv")
    res_x.read(tk)
    ag = sc.do("vector", lambda e: e.bn_aggr(mv[:, :], bn6[:, :]),
               waits=[tk], inc=nm + "_dv")
    if lvl < 3:
        return ag, ag
    t3 = sc.do("scalar", lambda e: e.activation(
        rstd[:, :], mv[:, 1:2], AF.Ln, bias=eps_sb[:, 0:1], scale=1.0),
        waits=[ag], inc=nm + "_ac")
    t4 = sc.do("scalar", lambda e: e.activation(
        rstd[:, :], rstd[:, :], AF.Exp, bias=0.0, scale=-0.5),
        waits=[t3], inc=nm + "_ac")
    res_stats.write(t4)
    return ag, t4


def build_nc():
    nc = bass.Bass("TRN2", target_bir_lowering=False)

    x_in = nc.dram_tensor("x", [S, D], F32, kind="ExternalInput").ap()
    wq = nc.dram_tensor("wq", [D, D], F32, kind="ExternalInput").ap()
    wk = nc.dram_tensor("wk", [D, D], F32, kind="ExternalInput").ap()
    wv = nc.dram_tensor("wv", [D, D], F32, kind="ExternalInput").ap()
    wo = nc.dram_tensor("wo", [D, D], F32, kind="ExternalInput").ap()
    wt = nc.dram_tensor("wt", [D, D], F32, kind="ExternalInput").ap()
    bia = nc.dram_tensor("bia", [1, D], F32, kind="ExternalInput").ap()
    lat = nc.dram_tensor("lat", [D, 96], F32, kind="ExternalInput").ap()
    lbt = nc.dram_tensor("lbt", [96, D], F32, kind="ExternalInput").ap()
    lsi = nc.dram_tensor("lsi", [1, 4], F32, kind="ExternalInput").ap()
    cb = nc.dram_tensor("cb", [1, 1], F32, kind="ExternalInput").ap()
    csq = nc.dram_tensor("csq", [128, S], F32, kind="ExternalInput").ap()
    csk = nc.dram_tensor("csk", [128, S], F32, kind="ExternalInput").ap()
    tri = nc.dram_tensor("tri", [128, 128], F32, kind="ExternalInput").ap()
    idn = nc.dram_tensor("idn", [128, 128], F32, kind="ExternalInput").ap()
    e96 = nc.dram_tensor("e96", [3, 96], F32, kind="ExternalInput").ap()
    c3 = nc.dram_tensor("c3", [3, 1], F32, kind="ExternalInput").ap()

    out = nc.dram_tensor("out", [S, D], F32, kind="ExternalOutput").ap()
    qt_d = nc.dram_tensor("qt_d", [D, S], BF16, kind="Internal").ap()
    kt_d = nc.dram_tensor("kt_d", [D, S], BF16, kind="Internal").ap()
    v_d = nc.dram_tensor("v_d", [S, D], BF16, kind="Internal").ap()
    ot_d = nc.dram_tensor("ot_d", [D, S], BF16, kind="Internal").ap()
    attn_d = nc.dram_tensor("attn_d", [S, D], F32, kind="Internal").ap()
    x2_d = nc.dram_tensor("x2_d", [S, D], F32, kind="Internal").ap()
    y_d = nc.dram_tensor("y_d", [S, D], F32, kind="Internal").ap()

    dbg = {}
    for k, shape, dt in (("lnt", [D, S], BF16), ("ot", [D, S], BF16),
                         ("attn", [S, D], F32), ("x2", [S, D], F32),
                         ("y", [S, D], F32), ("m96", [128, 96], BF16),
                         ("rt", [96, S], BF16), ("cc", [1, 8], F32),
                         ("m3", [3, 1], F32)):
        if k in DEBUG:
            dbg[k] = nc.dram_tensor("dbg_" + k, shape, dt,
                                    kind="ExternalOutput").ap()

    sc = Sched(nc)
    gx = ExitStack()
    gbuf = lambda n, s_, d_: gx.enter_context(nc.sbuf_tensor(n, s_, d_))

    idn_sb = gbuf("idn_sb", [128, 128], BF16)
    eps_sb = gbuf("eps_sb", [128, 1], F32)

    t_idn = sc.do("gpsimd", lambda g: g.dma_start(idn_sb[:, :], idn[:, :]),
                  inc="c_idn", dma=True)
    t_eps = sc.do("vector", lambda v: v.memset(eps_sb[:, :], EPS), inc="c_eps")

    def dbg_copy(key, src_ap, rows=None):
        if key in dbg:
            t = sc.do("sync", lambda e, k=key, s_=src_ap, rows=rows:
                      e.dma_start(dbg[k][0:rows, :] if rows else dbg[k][:, :],
                                  s_[0:rows, :] if rows else s_[:, :]),
                      inc="dbgd", dma=True)
            sc.do("sync", lambda e: None, waits=[t])

    ab_stack = ExitStack()
    ln1t = ab_stack.enter_context(nc.sbuf_tensor("ln1t", [128, NCH * 2048], BF16))
    bq_ps = ab_stack.enter_context(nc.psum_tensor("bq_ps", [128, 2048], F32))
    r_bps = [Res() for _ in range(4)]
    a_done = [None] * NT
    if "A" in PHASES:
        with ExitStack() as ph:
            pb = lambda n, s_, d_: ph.enter_context(nc.sbuf_tensor(n, s_, d_))
            x_f = [pb(f"a_x{j}", [128, D], F32) for j in range(2)]
            ln_bf = [pb(f"a_ln{j}", [128, D], BF16) for j in range(2)]
            bn6 = pb("a_bn6", [128, 48], F32)
            mv = pb("a_mv", [128, 2], F32)
            rstd = pb("a_rstd", [128, 1], F32)
            ps_t = [ph.enter_context(nc.psum_tensor(f"a_pst{jj_}", [128, 512], BF16))
                    for jj_ in range(2)]

            r_x = [Res() for _ in range(2)]
            r_ln = [Res() for _ in range(2)]
            r_st = Res()
            r_ps = [Res() for _ in range(2)]

            for i in range(NT):
                j = i % 2
                tk = sc.do("sync", lambda e, j=j, i=i: e.dma_start(
                    x_f[j][:, :], x_in[i * 128:(i + 1) * 128, :]),
                    waits=r_x[j].for_write(), inc=f"a_x{j}", dma=True)
                r_x[j].write(tk)
                _, trs = _ln_tile(sc, "a", x_f[j][:, :], bn6, mv, rstd,
                                  eps_sb, [t_eps], r_x[j], r_st)
                tn = sc.do("vector", lambda e, j=j: e.tensor_scalar(
                    ln_bf[j][:, :], x_f[j][:, :], mv[:, 0:1], rstd[:, 0:1],
                    ALU.subtract, ALU.mult),
                    waits=[trs] + r_ln[j].for_write(), inc="a_dv")
                r_x[j].read(tn)
                r_st.read(tn)
                r_ln[j].write(tn)
                ev = None
                for cg in range(8):
                    half = cg % 2
                    tw = [tn, t_idn] + r_ps[half].for_write()
                    tt = None
                    for cc in range(4):
                        c = cg * 4 + cc
                        tt = sc.do("tensor", lambda e, j=j, c=c, half=half, cc=cc:
                                   e.transpose(
                                       ps_t[half][:, cc * 128:(cc + 1) * 128],
                                       ln_bf[j][:, c * 128:(c + 1) * 128],
                                       idn_sb[:, :]),
                                   waits=tw if cc == 0 else [],
                                   inc="a_pe" if cc == 3 else None)
                    r_ln[j].read(tt)
                    r_ps[half].write(tt)

                    def evac(e, cg=cg, i=i, half=half):
                        dst = ln1t[:, :].rearrange(
                            "p (c t) -> p c t", c=NCH)[
                            :, cg * 4:(cg + 1) * 4, i * 128:(i + 1) * 128]
                        src = ps_t[half][:, :].rearrange(
                            "p (c t) -> p c t", c=4)[:, :, :]
                        return e.tensor_copy(dst, src)
                    ev = sc.do("vector", evac, waits=[tt], inc="a_dv")
                    r_ps[half].read(ev)
                a_done[i] = ev
            sc.flush("phaseA")

    b1_fin = []
    if "B1" in PHASES:
        with ExitStack() as ph:
            pb = lambda n, s_, d_: ph.enter_context(nc.sbuf_tensor(n, s_, d_))
            w_sb = [pb(f"b1_w{j}", [128, NCH * 128], BF16) for j in range(2)]
            stg = [pb(f"b1_s{j}", [128, 512], BF16) for j in range(2)]
            rot_f = pb("b1_rot", [64, 512], F32)
            sw_f = pb("b1_sw", [64, 512], F32)
            tmp_f = pb("b1_tmp", [64, 512], F32)
            cskc = pb("b1_cc", [64, S], F32)
            csks = pb("b1_cs", [64, S], F32)

            for t_, r0 in ((cskc, 0), (csks, 64)):
                sc.do("sync", lambda e, t_=t_, r0=r0: e.dma_start(
                    t_[:, :], csk[r0:r0 + 64, :]), inc="b1_tab", dma=True)
            t_tab1 = ("b1_tab", sc.cnt["b1_tab"])

            r_w = [Res() for _ in range(2)]
            r_stg = [Res() for _ in range(2)]
            r_rot, r_sw, r_tmp = Res(), Res(), Res()
            tmz = None
            for t_ in (rot_f, sw_f, tmp_f):
                tmz = sc.do("vector", lambda e, t_=t_: e.memset(t_[:, :], 0.0),
                            inc="b1_dv")
            r_rot.write(tmz)
            r_sw.write(tmz)
            r_tmp.write(tmz)

            blocks = ([("q", ob) for ob in range(32)]
                      + [("k", ob) for ob in range(32)])

            def load_w1(idx, waits):
                m, ob = blocks[idx]
                src = wq if m == "q" else wk
                j = idx % 2
                tok = None
                for hh in range(2):
                    def f(e, src=src, ob=ob, j=j, hh=hh):
                        s_ = src.rearrange("(c p) o -> p c o", p=128)[
                            :, hh * 16:(hh + 1) * 16, ob * 128:(ob + 1) * 128]
                        d_ = w_sb[j][:, :].rearrange(
                            "p (c o) -> p c o", c=NCH)[
                            :, hh * 16:(hh + 1) * 16, :]
                        return e.dma_start(d_, s_)
                    tok = sc.do("gpsimd", f, waits=waits if hh == 0 else [],
                                inc=f"b1_w{j}", dma=True)
                return tok

            t0 = load_w1(0, [])
            r_w[0].write(t0)
            gctr = [0]
            for idx in range(64):
                m, ob = blocks[idx]
                j = idx % 2
                sca = (1.0 / 16.0) if m == "q" else 1.0
                dst_d = qt_d if m == "q" else kt_d
                if idx + 1 < 64:
                    jn = (idx + 1) % 2
                    tn_ = load_w1(idx + 1, r_w[jn].for_write())
                    r_w[jn].write(tn_)
                for tt4 in range(4):
                    reg = gctr[0] % 4
                    sj = gctr[0] % 2
                    gctr[0] += 1
                    w_ = (r_w[j].for_read() + r_bps[reg].for_write()
                          + ([a_done[tt4 * 4 + 3]]
                             if a_done[tt4 * 4 + 3] is not None else []))
                    tmm = None
                    for c in range(NCH):
                        def f(e, j=j, c=c, tt4=tt4, reg=reg):
                            return e.matmul(
                                bq_ps[:, reg * 512:(reg + 1) * 512],
                                w_sb[j][:, :].rearrange(
                                    "p (c o) -> p c o", c=NCH)[:, c, :],
                                ln1t[:, :].rearrange(
                                    "p (c t) -> p c t", c=NCH)[
                                    :, c, tt4 * 512:(tt4 + 1) * 512],
                                start=(c == 0), stop=(c == NCH - 1))
                        tmm = sc.do("tensor", f, waits=w_ if c == 0 else [],
                                    inc="b1_pe" if c == NCH - 1 else None)
                    r_w[j].read(tmm)
                    r_bps[reg].write(tmm)
                    te = sc.do("scalar", lambda e, reg=reg, sj=sj, sca=sca:
                               e.activation(
                                   stg[sj][:, :],
                                   bq_ps[:, reg * 512:(reg + 1) * 512],
                                   AF.Copy, bias=0.0, scale=sca),
                               waits=[tmm] + r_stg[sj].for_write(),
                               inc="b1_ac")
                    r_bps[reg].read(te)
                    r_stg[sj].write(te)
                    dma_wait = [te]
                    if ob % 2 == 0:
                        tr_ = sc.do("scalar", lambda e, reg=reg, sca=sca:
                                    e.activation(
                                        rot_f[0:64, :],
                                        bq_ps[0:64, reg * 512:reg * 512 + 512],
                                        AF.Copy, bias=0.0, scale=sca),
                                    waits=[tmm] + r_rot.for_write(),
                                    inc="b1_ac")
                        r_rot.write(tr_)
                        r_bps[reg].read(tr_)
                        swmask = [i ^ 1 for i in range(32)]
                        swt = sc.do("vector", lambda e, swmask=swmask:
                                    e.stream_shuffle(
                                        sw_f[0:64, :], rot_f[0:64, :], swmask),
                                    waits=[tr_] + r_sw.for_write(),
                                    inc="b1_dv")
                        r_sw.write(swt)
                        r_rot.read(swt)
                        t1 = sc.do("vector", lambda e, tt4=tt4: e.tensor_mul(
                            tmp_f[0:64, :], rot_f[0:64, :],
                            cskc[0:64, tt4 * 512:(tt4 + 1) * 512]),
                            waits=[tr_, t_tab1] + r_tmp.for_write(),
                            inc="b1_dv")
                        r_tmp.write(t1)
                        r_rot.read(t1)
                        t2 = sc.do("vector", lambda e, tt4=tt4: e.tensor_mul(
                            sw_f[0:64, :], sw_f[0:64, :],
                            csks[0:64, tt4 * 512:(tt4 + 1) * 512]),
                            waits=[swt, t_tab1], inc="b1_dv")
                        r_sw.write(t2)
                        t3 = sc.do("vector", lambda e, sj=sj: e.tensor_add(
                            stg[sj][0:64, :], tmp_f[0:64, :], sw_f[0:64, :]),
                            waits=[t1, t2, te], inc="b1_dv")
                        r_tmp.read(t3)
                        r_sw.read(t3)
                        r_stg[sj].w.append(t3)
                        dma_wait = [te, t3]
                    ts_ = sc.do("sync", lambda e, sj=sj, dst_d=dst_d,
                                ob=ob, tt4=tt4: e.dma_start(
                                    dst_d[ob * 128:(ob + 1) * 128,
                                          tt4 * 512:(tt4 + 1) * 512],
                                    stg[sj][:, :]),
                                waits=dma_wait, inc=f"b1_st{reg}", dma=True)
                    r_stg[sj].read(ts_)
            sc.flush("phaseB1")
        b1_fin = [("b1_st0", sc.cnt["b1_st0"]), ("b1_st1", sc.cnt["b1_st1"])]

    if "B2" in PHASES:
        with ExitStack() as ph:
            pb = lambda n, s_, d_: ph.enter_context(nc.sbuf_tensor(n, s_, d_))
            wv_sb = [pb(f"b2_w{j}", [128, NCH * 512], BF16) for j in range(2)]
            vst = [pb(f"b2_s{j}", [128, 512], BF16) for j in range(2)]
            r_wv = [Res() for _ in range(2)]
            r_vst = [Res() for _ in range(2)]
            alias_w = list(b1_fin) + ([a_done[NT - 1]]
                                      if a_done[NT - 1] is not None else [])

            def load_wv(ob, j, waits):
                tok = None
                for hh in range(2):
                    def f(e, ob=ob, j=j, hh=hh):
                        src = wv.rearrange("(c p) o -> p c o", p=128)[
                            :, hh * 16:(hh + 1) * 16,
                            ob * 512:(ob + 1) * 512]
                        dst = wv_sb[j][:, :].rearrange(
                            "p (c o) -> p c o", c=NCH)[
                            :, hh * 16:(hh + 1) * 16, :]
                        return e.dma_start(dst, src)
                    tok = sc.do("gpsimd", f, waits=waits if hh == 0 else [],
                                inc=f"b2_w{j}", dma=True)
                return tok

            t0 = load_wv(0, 0, alias_w)
            r_wv[0].write(t0)
            gctr2 = [0]
            for ob in range(8):
                j = ob % 2
                if ob + 1 < 8:
                    jn = (ob + 1) % 2
                    tn_ = load_wv(ob + 1, jn, r_wv[jn].for_write()
                                  + (alias_w if ob == 0 else []))
                    r_wv[jn].write(tn_)
                for i in range(NT):
                    reg = gctr2[0] % 4
                    vj = i % 2
                    gctr2[0] += 1
                    w_ = (r_wv[j].for_read() + r_bps[reg].for_write()
                          + ([a_done[i]] if a_done[i] is not None else []))
                    tmm = None
                    for c in range(NCH):
                        def f(e, c=c, i=i, j=j, reg=reg):
                            return e.matmul(
                                bq_ps[:, reg * 512:(reg + 1) * 512],
                                ln1t[:, :].rearrange(
                                    "p (c t) -> p c t", c=NCH)[
                                    :, c, i * 128:(i + 1) * 128],
                                wv_sb[j][:, :].rearrange(
                                    "p (c o) -> p c o", c=NCH)[:, c, :],
                                start=(c == 0), stop=(c == NCH - 1))
                        tmm = sc.do("tensor", f, waits=w_ if c == 0 else [],
                                    inc="b2_pe" if c == NCH - 1 else None)
                    r_wv[j].read(tmm)
                    r_bps[reg].write(tmm)
                    te = sc.do("scalar", lambda e, reg=reg, vj=vj: e.activation(
                        vst[vj][:, :], bq_ps[:, reg * 512:(reg + 1) * 512],
                        AF.Copy, bias=0.0, scale=1.0),
                        waits=[tmm] + r_vst[vj].for_write()
                        + (alias_w if (ob == 0 and i == 0) else []),
                        inc="b2_ac")
                    r_bps[reg].read(te)
                    r_vst[vj].write(te)
                    ts_ = sc.do("sync", lambda e, vj=vj, i=i, ob=ob:
                                e.dma_start(
                                    v_d[i * 128:(i + 1) * 128,
                                        ob * 512:(ob + 1) * 512],
                                    vst[vj][:, :]),
                                waits=[te], inc=f"b2_st{reg}", dma=True)
                    r_vst[vj].read(ts_)
            sc.flush("phaseB2")

    sc.barrier()
    ab_stack.close()

    if "B3" in PHASES:
        with ExitStack() as ph:
            pb = lambda n, s_, d_: ph.enter_context(nc.sbuf_tensor(n, s_, d_))
            qT = [[pb(f"b3_q{u}{b}", [128, 2 * S], BF16) for b in range(2)]
                  for u in range(2)]
            kT = [[pb(f"b3_k{u}{b}", [128, 2 * S], BF16) for b in range(2)]
                  for u in range(2)]
            v_sb = [[pb(f"b3_v{u}{b}", [128, NT * 256], BF16)
                     for b in range(2)] for u in range(2)]
            P_sb = [pb(f"b3_P{u}", [128, 2 * S], BF16) for u in range(2)]
            PT_sb = [pb(f"b3_PT{u}", [128, 1024], BF16) for u in range(2)]
            O_acc = [pb(f"b3_O{u}", [128, NT * HD], BF16) for u in range(2)]
            OT_sb = [pb(f"b3_OT{u}", [128, 2 * S], BF16) for u in range(2)]
            mx = pb("b3_mx", [128, 8], F32)
            mneg2 = pb("b3_mneg", [128, 4], F32)
            lc = pb("b3_lc", [128, 16], F32)
            rl2 = pb("b3_rl", [128, 4], F32)
            tri_sb = pb("b3_tri", [128, 128], F32)
            ps_s = ph.enter_context(nc.psum_tensor("b3_pss", [128, 2048], F32))
            ps_o = [ph.enter_context(nc.psum_tensor(f"b3_pso{u}", [128, 512],
                                                    F32)) for u in range(2)]
            ps_p = [ph.enter_context(nc.psum_tensor(f"b3_psp{u}", [128, 512],
                                                    BF16)) for u in range(2)]

            sc.do("sync", lambda e: e.dma_start(tri_sb[:, :], tri[:, :]),
                  inc="b3_tab", dma=True)
            t_tab = ("b3_tab", sc.cnt["b3_tab"])

            r_q = [[Res() for _ in range(2)] for _ in range(2)]
            r_k = [[Res() for _ in range(2)] for _ in range(2)]
            r_v = [[Res() for _ in range(2)] for _ in range(2)]
            r_P = [[Res() for _ in range(2)] for _ in range(2)]
            r_PT = [[Res() for _ in range(2)] for _ in range(2)]
            r_Oacc = [Res() for _ in range(2)]
            r_OT = [Res() for _ in range(2)]
            r_mx = [Res() for _ in range(2)]
            r_lc = [[Res() for _ in range(2)] for _ in range(2)]
            r_mneg = [[Res() for _ in range(2)] for _ in range(2)]
            r_rl = [[Res() for _ in range(2)] for _ in range(2)]
            r_sreg = [Res() for _ in range(4)]
            r_pso = [Res() for _ in range(2)]
            r_psp = [Res() for _ in range(2)]
            s_ctr = [0]

            def load_pair(g):
                b = g % 2
                sem = f"b3_l{b}"
                for u in range(2):
                    h = 2 * g + u
                    wfull = (r_q[u][b].for_write() + r_k[u][b].for_write()
                             + r_v[u][b].for_write())
                    sc.do("sync", lambda e, h=h, u=u, b=b: e.dma_start(
                        qT[u][b][:, 0:S], qt_d[h * HD:h * HD + 128, :]),
                        waits=wfull, inc=sem, dma=True)
                    sc.do("sync", lambda e, h=h, u=u, b=b: e.dma_start(
                        qT[u][b][:, S:2 * S],
                        qt_d[h * HD + 128:h * HD + 256, :]),
                        inc=sem, dma=True)
                    sc.do("gpsimd", lambda e, h=h, u=u, b=b: e.dma_start(
                        kT[u][b][:, 0:S], kt_d[h * HD:h * HD + 128, :]),
                        waits=wfull, inc=sem, dma=True)
                    sc.do("gpsimd", lambda e, h=h, u=u, b=b: e.dma_start(
                        kT[u][b][:, S:2 * S],
                        kt_d[h * HD + 128:h * HD + 256, :]),
                        inc=sem, dma=True)

                    def fv(e, h=h, u=u, b=b):
                        src = v_d.rearrange("(t p) c -> p t c", p=128)[
                            :, :, h * HD:(h + 1) * HD]
                        dst = v_sb[u][b][:, :].rearrange(
                            "p (t c) -> p t c", t=NT)
                        return e.dma_start(dst, src)
                    sc.do("sync", fv, inc=sem, dma=True)
                tok = (sem, sc.cnt[sem])
                for u in range(2):
                    r_q[u][b].write(tok)
                    r_k[u][b].write(tok)
                    r_v[u][b].write(tok)

            def emit_scores(u, b, qi):
                rowlen = (qi + 1) * 128
                nchk = (rowlen + 511) // 512
                chks = []
                ldtok = r_q[u][b].for_read() + r_k[u][b].for_read()
                for ci in range(nchk):
                    k0 = ci * 512
                    klen = min(rowlen - k0, 512)
                    reg = s_ctr[0] % 4
                    s_ctr[0] += 1
                    off = reg * 512
                    chks.append((ci, k0, klen, off, reg))
                    w_ = r_sreg[reg].for_write() + ldtok
                    tt = None
                    for st in range(2):
                        def f(e, u=u, b=b, qi=qi, k0=k0, klen=klen, st=st,
                              off=off):
                            return e.matmul(
                                ps_s[:, off:off + klen],
                                qT[u][b][:, st * S + qi * 128:
                                         st * S + (qi + 1) * 128],
                                kT[u][b][:, st * S + k0:st * S + k0 + klen],
                                start=(st == 0), stop=(st == 1))
                        tt = sc.do("tensor", f, waits=w_ if st == 0 else [],
                                   inc="b3_pe" if st == 1 else None)
                    r_q[u][b].read(tt)
                    r_k[u][b].read(tt)
                    r_sreg[reg].write(tt)
                    if k0 + klen == rowlen:
                        dlo = off + klen - 128
                        tt = sc.do("vector", lambda e, dlo=dlo: e.tensor_add(
                            ps_s[:, dlo:dlo + 128], ps_s[:, dlo:dlo + 128],
                            tri_sb[:, :]),
                            waits=[tt, t_tab], inc="b3_dv")
                        r_sreg[reg].w.append(tt)
                    tm = sc.do("vector", lambda e, u=u, off=off, klen=klen,
                               ci=ci: e.reduce_max(
                                   mx[:, u * 4 + ci:u * 4 + ci + 1],
                                   ps_s[:, off:off + klen], AX.X),
                               waits=[tt] + (r_mx[u].for_write()
                                             if ci == 0 else []),
                               inc="b3_dv")
                    r_sreg[reg].read(tm)
                    if ci == 0:
                        r_mx[u].write(tm)
                    else:
                        r_mx[u].w.append(tm)
                pj = qi % 2
                tmn = sc.do("vector", lambda e, u=u, nchk=nchk, pj=pj:
                            e.reduce_max(
                                mneg2[:, u * 2 + pj:u * 2 + pj + 1],
                                mx[:, u * 4:u * 4 + nchk], AX.X, negate=True),
                            waits=r_mx[u].w + r_mneg[u][pj].for_write(),
                            inc="b3_dv")
                r_mx[u].read(tmn)
                r_mneg[u][pj].write(tmn)
                return {"u": u, "b": b, "qi": qi, "pj": pj, "tmn": tmn,
                        "chks": chks}

            def emit_softmax(st_):
                u, qi, pj, tmn, chks = (st_["u"], st_["qi"], st_["pj"],
                                        st_["tmn"], st_["chks"])
                nchk = len(chks)
                etoks = []
                for (ci, k0, klen, off, reg) in chks:
                    def fe(e, u=u, off=off, klen=klen, k0=k0, ci=ci, pj=pj):
                        return e.activation(
                            P_sb[u][:, pj * S + k0:pj * S + k0 + klen],
                            ps_s[:, off:off + klen], AF.Exp,
                            bias=mneg2[:, u * 2 + pj:u * 2 + pj + 1],
                            scale=1.0,
                            accum_out=lc[:, u * 8 + pj * 4 + ci:
                                         u * 8 + pj * 4 + ci + 1])
                    te = sc.do("scalar", fe,
                               waits=[tmn] + (r_P[u][pj].for_write()
                                              + r_lc[u][pj].for_write()
                                              if ci == 0 else []),
                               inc="b3_ac")
                    etoks.append(te)
                    r_sreg[reg].read(te)
                    r_mneg[u][pj].read(te)
                    if ci == 0:
                        r_P[u][pj].write(te)
                        r_lc[u][pj].write(te)
                    else:
                        r_P[u][pj].w.append(te)
                        r_lc[u][pj].w.append(te)

                tl = sc.do("vector", lambda e, u=u, nchk=nchk, pj=pj:
                           e.reduce_sum(
                               rl2[:, u * 2 + pj:u * 2 + pj + 1],
                               lc[:, u * 8 + pj * 4:u * 8 + pj * 4 + nchk],
                               AX.X),
                           waits=etoks + r_rl[u][pj].for_write(),
                           inc="b3_dv")
                r_lc[u][pj].read(tl)
                trl = sc.do("vector", lambda e, u=u, pj=pj: e.reciprocal(
                    rl2[:, u * 2 + pj:u * 2 + pj + 1],
                    rl2[:, u * 2 + pj:u * 2 + pj + 1]),
                    waits=[tl], inc="b3_dv")
                r_rl[u][pj].write(trl)
                st_["etoks"] = etoks
                st_["trl"] = trl
                return st_

            def emit_pv(st_):
                u, b, qi, pj = st_["u"], st_["b"], st_["qi"], st_["pj"]
                etoks = st_["etoks"]
                trl = st_["trl"]
                ngrp = (qi + 4) // 4
                pv_tok = None
                for gp in range(ngrp):
                    jlo = gp * 4
                    jhi = min(qi + 1, jlo + 4)
                    pbh = gp % 2
                    tt = None
                    for jj_ in range(jlo, jhi):
                        def ft(e, u=u, pj=pj, jj_=jj_):
                            return e.transpose(
                                ps_p[u][:, (jj_ % 4) * 128:
                                       (jj_ % 4 + 1) * 128],
                                P_sb[u][:, pj * S + jj_ * 128:
                                        pj * S + (jj_ + 1) * 128],
                                idn_sb[:, :])
                        tt = sc.do("tensor", ft,
                                   waits=([etoks[gp], t_idn]
                                          + r_psp[u].for_write())
                                   if jj_ == jlo else [],
                                   inc="b3_pe" if jj_ == jhi - 1 else None)
                    r_P[u][pj].read(tt)
                    r_psp[u].write(tt)

                    def fev(e, u=u, pbh=pbh, n=(jhi - jlo)):
                        return e.tensor_copy(
                            PT_sb[u][:, pbh * 512:pbh * 512 + n * 128],
                            ps_p[u][:, 0:n * 128])
                    tev = sc.do("vector", fev,
                                waits=[tt] + r_PT[u][pbh].for_write(),
                                inc="b3_dv")
                    r_psp[u].read(tev)
                    r_PT[u][pbh].write(tev)
                    for jj_ in range(jlo, jhi):
                        def fpv(e, u=u, b=b, pbh=pbh, jj_=jj_, qi=qi):
                            return e.matmul(
                                ps_o[u][:, 0:HD],
                                PT_sb[u][:, pbh * 512 + (jj_ % 4) * 128:
                                        pbh * 512 + (jj_ % 4 + 1) * 128],
                                v_sb[u][b][:, jj_ * 256:(jj_ + 1) * 256],
                                start=(jj_ == 0), stop=(jj_ == qi),
                                skip_group_check=True)
                        ww = [tev] + r_v[u][b].for_read()
                        if jj_ == 0:
                            ww = ww + r_pso[u].for_write()
                        pv_tok = sc.do("tensor", fpv,
                                       waits=ww if jj_ in (jlo, 0) else [],
                                       inc="b3_pe" if jj_ == jhi - 1
                                       else None)
                    r_PT[u][pbh].read(pv_tok)
                r_v[u][b].read(pv_tok)
                r_pso[u].write(pv_tok)

                tof = sc.do("vector", lambda e, u=u, pj=pj, qi=qi:
                            e.tensor_scalar(
                                O_acc[u][:, qi * HD:(qi + 1) * HD],
                                ps_o[u][:, 0:HD],
                                rl2[:, u * 2 + pj:u * 2 + pj + 1], None,
                                ALU.mult),
                            waits=[pv_tok, trl]
                            + (r_Oacc[u].for_write() if qi == 0 else []),
                            inc="b3_dv")
                r_rl[u][pj].read(tof)
                r_pso[u].read(tof)
                if qi == 0:
                    r_Oacc[u].write(tof)
                else:
                    r_Oacc[u].w.append(tof)

            load_pair(0)
            for g in range(8):
                b = g % 2
                prevA = prevB = None
                for qi in range(NT):
                    curA = emit_scores(0, b, qi)
                    if qi == 1 and g + 1 < 8:
                        load_pair(g + 1)
                    if prevB is not None:
                        emit_pv(prevB)
                    curA = emit_softmax(curA)
                    curB = emit_scores(1, b, qi)
                    if prevA is not None:
                        emit_pv(prevA)
                    curB = emit_softmax(curB)
                    prevA, prevB = curA, curB
                emit_pv(prevA)
                emit_pv(prevB)
                for u in range(2):
                    for st in range(2):
                        for grp in range(4):
                            tt = None
                            for k in range(4):
                                qi_ = grp * 4 + k

                                def ftr(e, u=u, st=st, qi_=qi_, k=k):
                                    return e.transpose(
                                        ps_p[u][:, k * 128:(k + 1) * 128],
                                        O_acc[u][:, qi_ * HD + st * 128:
                                               qi_ * HD + (st + 1) * 128],
                                        idn_sb[:, :])
                                tt = sc.do("tensor", ftr,
                                           waits=([r_Oacc[u].w[-1], t_idn]
                                                  + r_psp[u].for_write())
                                           if k == 0 else [],
                                           inc="b3_pe" if k == 3 else None)
                            r_Oacc[u].read(tt)
                            r_psp[u].write(tt)

                            def fevo(e, u=u, st=st, grp=grp):
                                return e.tensor_copy(
                                    OT_sb[u][:, st * S + grp * 512:
                                           st * S + (grp + 1) * 512],
                                    ps_p[u][:, 0:512])
                            tev = sc.do("vector", fevo,
                                        waits=[tt] + (r_OT[u].for_write()
                                                      if (st == 0
                                                          and grp == 0)
                                                      else []),
                                        inc="b3_dv")
                            r_psp[u].read(tev)
                            if st == 0 and grp == 0:
                                r_OT[u].write(tev)
                            else:
                                r_OT[u].w.append(tev)
                for u in range(2):
                    h = 2 * g + u
                    st_tok = None
                    for s_ in range(2):
                        def fst(e, u=u, s_=s_, h=h):
                            return e.dma_start(
                                ot_d[h * HD + s_ * 128:
                                     h * HD + (s_ + 1) * 128, :],
                                OT_sb[u][:, s_ * S:(s_ + 1) * S])
                        st_tok = sc.do("sync", fst,
                                       waits=r_OT[u].for_write()
                                       if s_ == 0 else [],
                                       inc="b3_otd", dma=True)
                    r_OT[u].read(st_tok)

            sc.barrier()
            dbg_copy("ot", ot_d)
            sc.barrier()
            sc.flush("phaseB3")


    if "C" in PHASES:
        with ExitStack() as ph:
            pb = lambda n, s_, d_: ph.enter_context(nc.sbuf_tensor(n, s_, d_))
            ot_sb = pb("c_ot", [128, NCH * 2048], BF16)
            wo_sb = [pb(f"c_wo{j}", [128, NCH * 512], BF16) for j in range(2)]
            at_sb = [pb(f"c_at{j}", [128, 512], F32) for j in range(2)]
            ps_c = ph.enter_context(nc.psum_tensor("c_ps", [128, 1024], F32))

            r_ot, r_wo = Res(), [Res() for _ in range(2)]
            r_at = [Res() for _ in range(2)]
            r_psc = [Res() for _ in range(2)]

            tok = None
            for qd in range(4):
                def f(e, qd=qd):
                    src = ot_d.rearrange("(c p) t -> p c t", p=128)[
                        :, qd * 8:(qd + 1) * 8, :]
                    dst = ot_sb[:, :].rearrange("p (c t) -> p c t", c=NCH)[
                        :, qd * 8:(qd + 1) * 8, :]
                    return e.dma_start(dst, src)
                tok = sc.do("sync", f, inc="c_otl", dma=True)
            r_ot.write(tok)

            def load_wo(ob, j, waits):
                tok = None
                for hh in range(2):
                    def f(e, ob=ob, j=j, hh=hh):
                        src = wo.rearrange("(c p) o -> p c o", p=128)[
                            :, hh * 16:(hh + 1) * 16,
                            ob * 512:(ob + 1) * 512]
                        dst = wo_sb[j][:, :].rearrange(
                            "p (c o) -> p c o", c=NCH)[
                            :, hh * 16:(hh + 1) * 16, :]
                        return e.dma_start(dst, src)
                    tok = sc.do("gpsimd", f, waits=waits if hh == 0 else [],
                                inc=f"c_wo{j}d", dma=True)
                return tok

            t0 = load_wo(0, 0, [])
            r_wo[0].write(t0)
            for ob in range(8):
                j = ob % 2
                if ob + 1 < 8:
                    jn = (ob + 1) % 2
                    tn_ = load_wo(ob + 1, jn, r_wo[jn].for_write())
                    r_wo[jn].write(tn_)
                for i in range(NT):
                    reg = i % 2
                    w_ = (r_wo[j].for_read() + r_ot.for_read()
                          + r_psc[reg].for_write())
                    tt = None
                    for c in range(NCH):
                        def f(e, c=c, i=i, j=j, reg=reg):
                            return e.matmul(
                                ps_c[:, reg * 512:reg * 512 + 512],
                                ot_sb[:, :].rearrange(
                                    "p (c t) -> p c t", c=NCH)[
                                    :, c, i * 128:(i + 1) * 128],
                                wo_sb[j][:, :].rearrange(
                                    "p (c o) -> p c o", c=NCH)[:, c, :],
                                start=(c == 0), stop=(c == NCH - 1))
                        tt = sc.do("tensor", f, waits=w_ if c == 0 else [],
                                   inc="c_pe" if c == NCH - 1 else None)
                    r_wo[j].read(tt)
                    r_ot.read(tt)
                    r_psc[reg].write(tt)

                    def ev(e, reg=reg):
                        return e.activation(
                            at_sb[reg][:, :], ps_c[:, reg * 512:reg * 512 + 512],
                            AF.Copy, bias=0.0, scale=1.0)
                    te = sc.do("scalar", ev,
                               waits=[tt] + r_at[reg].for_write(), inc="c_ac")
                    r_psc[reg].read(te)
                    r_at[reg].write(te)

                    def fst(e, reg=reg, i=i, ob=ob):
                        return e.dma_start(
                            attn_d[i * 128:(i + 1) * 128,
                                   ob * 512:(ob + 1) * 512],
                            at_sb[reg][:, :])
                    ts_ = sc.do("sync", fst, waits=[te], inc=f"c_std{reg}",
                                dma=True)
                    r_at[reg].read(ts_)

            sc.barrier()
            dbg_copy("attn", attn_d)
            sc.barrier()
            sc.flush("phaseC")

    ln3_stack = ExitStack()
    if "C2" in PHASES:
        ln3t = ln3_stack.enter_context(
            nc.sbuf_tensor("d_ln3t", [128, NCH * 2048], BF16))
        r_ln3t = Res()
        with ExitStack() as ph:
            pb = lambda n, s_, d_: ph.enter_context(nc.sbuf_tensor(n, s_, d_))
            x_f = [pb(f"g_x{j}", [128, D], F32) for j in range(2)]
            a_f = [pb(f"g_a{j}", [128, D], F32) for j in range(2)]
            ln_bf = pb("g_ln", [128, D], BF16)
            bn6 = pb("g_bn6", [128, 48], F32)
            mv = pb("g_mv", [128, 2], F32)
            rstd = pb("g_rstd", [128, 1], F32)
            ps_t = [ph.enter_context(nc.psum_tensor(f"g_pst{jj_}", [128, 512], BF16))
                    for jj_ in range(2)]

            r_x = [Res() for _ in range(2)]
            r_a = [Res() for _ in range(2)]
            r_ln = Res()
            r_st = Res()
            r_ps = [Res() for _ in range(2)]

            for i in range(NT):
                j = i % 2
                tx = sc.do("sync", lambda e, j=j, i=i: e.dma_start(
                    x_f[j][:, :], x_in[i * 128:(i + 1) * 128, :]),
                    waits=r_x[j].for_write(), inc=f"g_x{j}", dma=True)
                r_x[j].write(tx)
                ta = sc.do("sync", lambda e, j=j, i=i: e.dma_start(
                    a_f[j][:, :], attn_d[i * 128:(i + 1) * 128, :]),
                    waits=r_a[j].for_write(), inc=f"g_a{j}", dma=True)
                r_a[j].write(ta)

                _, trs = _ln_tile(sc, "g", a_f[j][:, :], bn6, mv, rstd,
                                  eps_sb, [t_eps], r_a[j], r_st)
                tn = sc.do("vector", lambda e, j=j: e.tensor_scalar(
                    a_f[j][:, :], a_f[j][:, :], mv[:, 0:1], rstd[:, 0:1],
                    ALU.subtract, ALU.mult), waits=[trs], inc="g_dv")
                r_a[j].write(tn)
                r_st.read(tn)
                tn2 = sc.do("vector", lambda e, j=j: e.tensor_add(
                    x_f[j][:, :], x_f[j][:, :], a_f[j][:, :]),
                    waits=[tn, tx], inc="g_dv")
                r_x[j].write(tn2)
                r_a[j].read(tn2)
                ts_ = sc.do("sync", lambda e, j=j, i=i: e.dma_start(
                    x2_d[i * 128:(i + 1) * 128, :], x_f[j][:, :]),
                    waits=[tn2], inc=f"g_x2d{j}", dma=True)
                r_x[j].read(ts_)
                _, trs3 = _ln_tile(sc, "g3", x_f[j][:, :], bn6, mv, rstd,
                                   eps_sb, [], r_x[j], r_st)
                tn3 = sc.do("vector", lambda e, j=j: e.tensor_scalar(
                    ln_bf[:, :], x_f[j][:, :], mv[:, 0:1], rstd[:, 0:1],
                    ALU.subtract, ALU.mult),
                    waits=[trs3] + r_ln.for_write(), inc="g_dv")
                r_x[j].read(tn3)
                r_st.read(tn3)
                r_ln.write(tn3)
                for cg in range(8):
                    half = cg % 2
                    tw = [tn3, t_idn] + r_ps[half].for_write()
                    tt = None
                    for cc in range(4):
                        c = cg * 4 + cc
                        tt = sc.do("tensor", lambda e, c=c, half=half, cc=cc:
                                   e.transpose(
                                       ps_t[half][:, cc * 128:(cc + 1) * 128],
                                       ln_bf[:, c * 128:(c + 1) * 128],
                                       idn_sb[:, :]),
                                   waits=tw if cc == 0 else [],
                                   inc="g_pe" if cc == 3 else None)
                    r_ln.read(tt)
                    r_ps[half].write(tt)

                    def evac(e, cg=cg, i=i, half=half):
                        dst = ln3t[:, :].rearrange(
                            "p (c t) -> p c t", c=NCH)[
                            :, cg * 4:(cg + 1) * 4, i * 128:(i + 1) * 128]
                        src = ps_t[half][:, :].rearrange(
                            "p (c t) -> p c t", c=4)[:, :, :]
                        return e.tensor_copy(dst, src)
                    ev = sc.do("vector", evac, waits=[tt], inc="g_dv")
                    r_ps[half].read(ev)
                    r_ln3t.w.append(ev)

            sc.barrier()
            dbg_copy("x2", x2_d)
            sc.barrier()
            sc.flush("phaseC2")

    if "D" in PHASES:
        with ExitStack() as ph:
            pb = lambda n, s_, d_: ph.enter_context(nc.sbuf_tensor(n, s_, d_))
            rt_sb = pb("d_rt", [96, S], BF16)
            bia_s = pb("d_bias", [1, 256], F32)
            ones1 = pb("d_ones1", [1, 128], F32)
            bb_sb = pb("d_bb", [128, 256], F32)
            bts = [pb(f"d_bt{j}", [96, 256], BF16) for j in range(2)]
            y_sb = [pb(f"d_y{j}", [128, 256], F32) for j in range(2)]
            ps_d = ph.enter_context(nc.psum_tensor("d_ps", [128, 2048], F32))
            ps_s = ph.enter_context(nc.psum_tensor("d_pss", [128, 512], F32))
            ps_r = ph.enter_context(nc.psum_tensor("d_psr", [128, 128], BF16))
            bnall = pb("d_bnall", [128, NT * 96], F32)
            e_mv = pb("d_emv", [128, 2], F32)
            e_rstd = pb("d_erstd", [128, 1], F32)
            yc = [pb(f"d_yc{k}", [128, 1024], F32) for k in range(3)]
            xc = [pb(f"d_xc{k}", [128, 1024], F32) for k in range(3)]
            r_yc = [Res() for _ in range(3)]
            r_xc = [Res() for _ in range(3)]
            r_est = Res()
            ph1 = ExitStack()
            pb1 = lambda n, s_, d_: ph1.enter_context(nc.sbuf_tensor(n, s_, d_))
            lsi_sb = pb1("d_lsi", [1, 4], F32)
            cb_sb = pb1("d_cb", [1, 1], F32)
            cc_sb = pb1("d_cc", [1, 8], F32)
            ones3 = pb1("d_ones3", [3, 128], F32)
            c3_sb = pb1("d_c3", [3, 1], F32)
            m3 = pb1("d_m3", [3, 1], F32)
            e96_sb = pb1("d_e96", [3, 96], F32)
            me_sb = pb1("d_me", [3, 96], F32)
            m96 = pb1("d_m96", [128, 96], BF16)
            at_sb = pb1("d_at", [128, NCH * 96], BF16)
            rtmp = pb1("d_rtmp", [128, 96], BF16)

            tl1 = sc.do("sync", lambda e: e.dma_start(lsi_sb[:, :], lsi[:, :]),
                        inc="d_cl", dma=True)
            tl2 = sc.do("sync", lambda e: e.dma_start(cb_sb[:, :], cb[:, :]),
                        inc="d_cl", dma=True)
            tl3 = sc.do("sync", lambda e: e.dma_start(c3_sb[:, :], c3[:, :]),
                        inc="d_cl", dma=True)
            tl4 = sc.do("sync", lambda e: e.dma_start(e96_sb[:, :], e96[:, :]),
                        inc="d_cl", dma=True)
            tcl = ("d_cl", sc.cnt["d_cl"])
            tl1 = tl2 = tl3 = tl4 = tcl

            tat = sc.do("gpsimd", lambda e: e.dma_start(
                at_sb[:, :].rearrange("p (c j) -> p c j", c=NCH),
                lat.rearrange("(c p) j -> p c j", p=128)),
                inc="d_atl", dma=True)

            to1 = sc.do("vector", lambda e: e.memset(ones3[:, :], 1.0),
                        inc="d_dv")
            to2 = sc.do("vector", lambda e: e.memset(ones1[:, :], 1.0),
                        inc="d_dv")
            t1 = sc.do("vector", lambda e: e.tensor_single_scalar(
                cc_sb[0:1, 0:4], lsi_sb[0:1, 0:4], cb_sb[0:1, 0:1], ALU.is_gt),
                waits=[tl1, tl2], inc="d_dv")
            t2 = sc.do("vector", lambda e: e.tensor_mul(
                cc_sb[0:1, 4:5], cc_sb[0:1, 0:1], cc_sb[0:1, 1:2]),
                waits=[t1], inc="d_dv")
            t3 = sc.do("vector", lambda e: e.tensor_add(
                cc_sb[0:1, 5:6], cc_sb[0:1, 1:2], cc_sb[0:1, 4:5]),
                waits=[t2], inc="d_dv")
            t4 = sc.do("vector", lambda e: e.tensor_scalar(
                cc_sb[0:1, 5:6], cc_sb[0:1, 5:6], 2.0, -1.0,
                ALU.subtract, ALU.mult), waits=[t3], inc="d_dv")
            t5 = sc.do("vector", lambda e: e.tensor_mul(
                cc_sb[0:1, 5:6], cc_sb[0:1, 5:6], cc_sb[0:1, 2:3]),
                waits=[t4], inc="d_dv")
            t6 = sc.do("vector", lambda e: e.tensor_scalar(
                cc_sb[0:1, 6:7], cc_sb[0:1, 2:3], 1.0, -1.0,
                ALU.subtract, ALU.mult), waits=[t1], inc="d_dv")
            t7 = sc.do("vector", lambda e: e.tensor_scalar(
                cc_sb[0:1, 7:8], cc_sb[0:1, 3:4], 4.0, -1.0,
                ALU.subtract, ALU.mult), waits=[t1], inc="d_dv")
            t8 = sc.do("vector", lambda e: e.tensor_mul(
                cc_sb[0:1, 6:7], cc_sb[0:1, 6:7], cc_sb[0:1, 7:8]),
                waits=[t6, t7], inc="d_dv")
            t9 = sc.do("vector", lambda e: e.tensor_add(
                cc_sb[0:1, 6:7], cc_sb[0:1, 6:7], cc_sb[0:1, 5:6]),
                waits=[t5, t8], inc="d_dv")
            tb1 = sc.do("tensor", lambda e: e.matmul(
                ps_s[0:3, 0:1], ones1[0:1, 0:3], cc_sb[0:1, 6:7],
                start=True, stop=True), waits=[t9, to2], inc="d_pe")
            t10 = sc.do("vector", lambda e: e.tensor_sub(
                m3[0:3, 0:1], ps_s[0:3, 0:1], c3_sb[0:3, 0:1]),
                waits=[tb1, tl3], inc="d_dv")
            t11 = sc.do("scalar", lambda e: e.activation(
                m3[0:3, 0:1], m3[0:3, 0:1], AF.Square, bias=0.0, scale=1.0),
                waits=[t10], inc="d_ac")
            t12 = sc.do("vector", lambda e: e.tensor_scalar(
                m3[0:3, 0:1], m3[0:3, 0:1], 1.0, -1.0,
                ALU.subtract, ALU.mult), waits=[t11], inc="d_dv")
            t13 = sc.do("vector", lambda e: e.tensor_relu(
                m3[0:3, 0:1], m3[0:3, 0:1]), waits=[t12], inc="d_dv")
            t14 = sc.do("vector", lambda e: e.tensor_scalar(
                me_sb[0:3, :], e96_sb[0:3, :], m3[0:3, 0:1], None, ALU.mult),
                waits=[t13, tl4], inc="d_dv")
            tb2 = sc.do("tensor", lambda e: e.matmul(
                ps_s[:, 0:96], ones3[0:3, :], me_sb[0:3, :],
                start=True, stop=True), waits=[t14, to1], inc="d_pe")
            t15 = sc.do("vector", lambda e: e.tensor_copy(
                m96[:, :], ps_s[:, 0:96]), waits=[tb2], inc="d_dv")
            tat_s = []
            for c in range(NCH):
                tt = sc.do("vector", lambda e, c=c: e.tensor_mul(
                    at_sb[:, c * 96:(c + 1) * 96],
                    at_sb[:, c * 96:(c + 1) * 96], m96[:, :]),
                    waits=[t15, tat] if c == 0 else [], inc="d_dv")
                tat_s.append(tt)
            r_rt = Res()
            r_pss = Res()
            r_psr = Res()
            r_psd = [Res() for _ in range(4)]
            for i in range(NT):
                tt = None
                for c in range(NCH):
                    def f(e, c=c, i=i):
                        return e.matmul(
                            ps_s[:, 0:96],
                            ln3t[:, :].rearrange("p (c t) -> p c t", c=NCH)[
                                :, c, i * 128:(i + 1) * 128],
                            at_sb[:, c * 96:(c + 1) * 96],
                            start=(c == 0), stop=(c == NCH - 1))
                    ww = [tat_s[-1], t15, tb2] if i == 0 else []
                    if i > 0:
                        ww = r_pss.for_write()
                    tt = sc.do("tensor", f, waits=ww if c == 0 else [],
                               inc="d_pe" if c == NCH - 1 else None)
                r_pss.write(tt)
                te = sc.do("vector", lambda e: e.tensor_copy(
                    rtmp[:, :], ps_s[:, 0:96]), waits=[tt], inc="d_dv")
                r_pss.read(te)
                ttp = sc.do("tensor", lambda e: e.transpose(
                    ps_r[0:96, :], rtmp[:, :], idn_sb[:, :]),
                    waits=[te, t_idn] + r_psr.for_write(), inc="d_pe")
                r_psr.write(ttp)
                tev = sc.do("vector", lambda e, i=i: e.tensor_copy(
                    rt_sb[0:96, i * 128:(i + 1) * 128], ps_r[0:96, :]),
                    waits=[ttp], inc="d_dv")
                r_psr.read(tev)
                r_rt.w.append(tev)

            sc.barrier()
            if "m96" in dbg:
                tdm = sc.do("sync", lambda e: e.dma_start(
                    dbg["m96"][:, :], m96[:, :]), inc="dbgd", dma=True)
                sc.do("sync", lambda e: None, waits=[tdm])
                if "cc" in dbg:
                    td2 = sc.do("sync", lambda e: e.dma_start(
                        dbg["cc"][:, :], cc_sb[:, :]), inc="dbgd", dma=True)
                    td3 = sc.do("sync", lambda e: e.dma_start(
                        dbg["m3"][:, :], m3[:, :]), inc="dbgd", dma=True)
                    sc.do("sync", lambda e: None, waits=[td3])
                sc.barrier()
            sc.flush("phaseD1")
            ph1.close()
            wt_sb = [pb(f"d_wt{j}", [128, NCH * 256], BF16) for j in range(2)]
            r_wt = [Res() for _ in range(2)]
            r_bt = [Res() for _ in range(2)]
            r_bia = Res()
            r_bb = Res()
            r_y = [Res() for _ in range(2)]

            def load_wt(ob, j, waits):
                tok = None
                for hh in range(2):
                    def f(e, ob=ob, j=j, hh=hh):
                        src = wt.rearrange("(c p) o -> p c o", p=128)[
                            :, hh * 16:(hh + 1) * 16,
                            ob * 256:(ob + 1) * 256]
                        dst = wt_sb[j][:, :].rearrange(
                            "p (c o) -> p c o", c=NCH)[
                            :, hh * 16:(hh + 1) * 16, :]
                        return e.dma_start(dst, src)
                    tok = sc.do("gpsimd", f, waits=waits if hh == 0 else [],
                                inc=f"d_wt{j}", dma=True)
                return tok

            def load_bt(ob, j, waits):
                return sc.do("gpsimd", lambda e, ob=ob, j=j: e.dma_start(
                    bts[j][:, :], lbt[0:96, ob * 256:(ob + 1) * 256]),
                    waits=waits, inc=f"d_bt{j}", dma=True)

            t0 = load_wt(0, 0, [])
            r_wt[0].write(t0)
            tb0 = load_bt(0, 0, [])
            r_bt[0].write(tb0)
            steps = [(ih, ob) for ih in range(2) for ob in range(16)]
            for si, (ih, ob) in enumerate(steps):
                j = si % 2
                if si + 1 < 32:
                    jn = (si + 1) % 2
                    tn_ = load_wt(steps[si + 1][1], jn, r_wt[jn].for_write())
                    r_wt[jn].write(tn_)
                    tnb = load_bt(steps[si + 1][1], jn, r_bt[jn].for_write())
                    r_bt[jn].write(tnb)
                tl5 = sc.do("sync", lambda e, ob=ob: e.dma_start(
                    bia_s[:, :], bia[0:1, ob * 256:(ob + 1) * 256]),
                    waits=r_bia.for_write(), inc="d_bl", dma=True)
                r_bia.write(tl5)
                tbb = sc.do("tensor", lambda e: e.matmul(
                    ps_s[:, 0:256], ones1[0:1, :], bia_s[0:1, :],
                    start=True, stop=True),
                    waits=[tl5, to2] + r_pss.for_write(), inc="d_pe")
                r_bia.read(tbb)
                r_pss.write(tbb)
                tbe = sc.do("vector", lambda e: e.tensor_copy(
                    bb_sb[:, :], ps_s[:, 0:256]),
                    waits=[tbb] + r_bb.for_write(), inc="d_dv")
                r_pss.read(tbe)
                r_bb.write(tbe)
                for ii in range(8):
                    i = ih * 8 + ii
                    reg = (si * 8 + ii) % 4
                    w_ = (r_wt[j].for_read() + r_psd[reg].for_write()
                          + (r_rt.w[-1:] if si == 0 and ii == 0 else []))
                    tt = None
                    for c in range(NCH):
                        def f(e, c=c, i=i, j=j, reg=reg):
                            return e.matmul(
                                ps_d[:, reg * 512:reg * 512 + 256],
                                ln3t[:, :].rearrange(
                                    "p (c t) -> p c t", c=NCH)[
                                    :, c, i * 128:(i + 1) * 128],
                                wt_sb[j][:, :].rearrange(
                                    "p (c o) -> p c o", c=NCH)[:, c, :],
                                start=(c == 0), stop=False)
                        tt = sc.do("tensor", f, waits=w_ if c == 0 else [])
                    def flo(e, i=i, j=j, reg=reg):
                        return e.matmul(
                            ps_d[:, reg * 512:reg * 512 + 256],
                            rt_sb[0:96, i * 128:(i + 1) * 128],
                            bts[j][0:96, :],
                            start=False, stop=True)
                    tt = sc.do("tensor", flo,
                               waits=r_bt[j].for_read() if i == 0 else [],
                               inc="d_pe")
                    r_bt[j].read(tt)
                    r_wt[j].read(tt)
                    r_psd[reg].write(tt)

                    yj = i % 2
                    def ev(e, reg=reg, yj=yj):
                        return e.tensor_add(
                            y_sb[yj][:, :], ps_d[:, reg * 512:reg * 512 + 256],
                            bb_sb[:, :])
                    te = sc.do("vector", ev,
                               waits=[tt, tbe] + r_y[yj].for_write(),
                               inc="d_dv")
                    r_psd[reg].read(te)
                    r_bb.read(te)
                    r_y[yj].write(te)

                    def fst(e, yj=yj, i=i, ob=ob):
                        return e.dma_start(
                            y_d[i * 128:(i + 1) * 128,
                                ob * 256:(ob + 1) * 256],
                            y_sb[yj][:, :])
                    ts_ = sc.do("sync", fst, waits=[te], inc=f"d_yd{yj}",
                                dma=True)
                    r_y[yj].read(ts_)
                    tbn = sc.do("vector", lambda e, yj=yj, i=i, ob=ob:
                                e.bn_stats(
                                    bnall[:, i * 96 + ob * 6:
                                          i * 96 + ob * 6 + 6],
                                    y_sb[yj][:, :]),
                                waits=[te], inc="d_dv")
                    r_y[yj].read(tbn)

                    if ob == 15:
                        tag = sc.do("vector", lambda e, i=i: e.bn_aggr(
                            e_mv[:, :], bnall[:, i * 96:i * 96 + 96]),
                            waits=[tbn] + r_est.for_write(), inc="d_dv")
                        t3_ = sc.do("scalar", lambda e: e.activation(
                            e_rstd[:, :], e_mv[:, 1:2], AF.Ln,
                            bias=eps_sb[:, 0:1], scale=1.0),
                            waits=[tag, t_eps], inc="d_ac")
                        trs_e = sc.do("scalar", lambda e: e.activation(
                            e_rstd[:, :], e_rstd[:, :], AF.Exp,
                            bias=0.0, scale=-0.5), waits=[t3_], inc="d_ac")
                        r_est.write(trs_e)
                        for cb in range(4):
                            kb = cb % 3
                            tyl = sc.do("scalar", lambda e, i=i, cb=cb, kb=kb:
                                        e.dma_start(
                                            yc[kb][:, :],
                                            y_d[i * 128:(i + 1) * 128,
                                                cb * 1024:(cb + 1) * 1024]),
                                        waits=[ts_] + r_yc[kb].for_write(),
                                        inc=f"d_ely{kb}", dma=True)
                            txl = sc.do("gpsimd", lambda e, i=i, cb=cb, kb=kb:
                                        e.dma_start(
                                            xc[kb][:, :],
                                            x2_d[i * 128:(i + 1) * 128,
                                                 cb * 1024:(cb + 1) * 1024]),
                                        waits=r_xc[kb].for_write(),
                                        inc=f"d_elx{kb}", dma=True)
                            v1 = sc.do("vector", lambda e, kb=kb:
                                       e.tensor_scalar(
                                           yc[kb][:, :], yc[kb][:, :],
                                           e_mv[:, 0:1], e_rstd[:, 0:1],
                                           ALU.subtract, ALU.mult),
                                       waits=[tyl, trs_e], inc="d_dv")
                            r_yc[kb].write(v1)
                            r_est.read(v1)
                            v2 = sc.do("vector", lambda e, kb=kb:
                                       e.tensor_add(
                                           xc[kb][:, :], xc[kb][:, :],
                                           yc[kb][:, :]),
                                       waits=[v1, txl], inc="d_dv")
                            r_yc[kb].read(v2)
                            r_xc[kb].write(v2)
                            tso = sc.do("sync", lambda e, i=i, cb=cb, kb=kb:
                                        e.dma_start(
                                            out[i * 128:(i + 1) * 128,
                                                cb * 1024:(cb + 1) * 1024],
                                            xc[kb][:, :]),
                                        waits=[v2], inc=f"d_out{kb}",
                                        dma=True)
                            r_xc[kb].read(tso)

            sc.do("sync", lambda e: None,
                  waits=[("d_out0", sc.cnt["d_out0"]),
                         ("d_out1", sc.cnt["d_out1"]),
                         ("d_out2", sc.cnt["d_out2"])])
            sc.barrier()
            if "rt" in dbg:
                tdr = sc.do("sync", lambda e: e.dma_start(
                    dbg["rt"][:, :], rt_sb[:, :]), inc="dbgd", dma=True)
                sc.do("sync", lambda e: None, waits=[tdr])
            dbg_copy("y", y_d)
            sc.barrier()
            sc.flush("phaseD")
    ln3_stack.close()

    gx.close()
    return nc


def kernel(
    L0_attn_hidden_states,
    L0_attn_Wq,
    L0_attn_Wk,
    L0_attn_Wv,
    L0_attn_Wo,
    L0_ffn_weight,
    L0_ffn_bias,
    L0_ffn_lora_A,
    L0_ffn_lora_B,
    L0_ffn_lora_start_indices,
):
    consts = _host_consts()
    x = np.ascontiguousarray(np.asarray(L0_attn_hidden_states, dtype=np.float32))
    wq_ = np.ascontiguousarray(np.asarray(L0_attn_Wq, dtype=np.float32))
    wk_ = np.ascontiguousarray(np.asarray(L0_attn_Wk, dtype=np.float32))
    wv_ = np.ascontiguousarray(np.asarray(L0_attn_Wv, dtype=np.float32))
    wo_ = np.ascontiguousarray(np.asarray(L0_attn_Wo, dtype=np.float32))
    wt_ = np.ascontiguousarray(np.asarray(L0_ffn_weight, dtype=np.float32).T)
    bia_ = np.ascontiguousarray(
        np.asarray(L0_ffn_bias, dtype=np.float32))[None, :]
    la = np.asarray(L0_ffn_lora_A, dtype=np.float32)
    lb = np.asarray(L0_ffn_lora_B, dtype=np.float32)
    lat_ = np.zeros((D, 96), dtype=np.float32)
    lbt_ = np.zeros((96, D), dtype=np.float32)
    for n in range(3):
        lat_[:, n * 32: n * 32 + 16] = la[n].T
        lbt_[n * 32: n * 32 + 16, :] = lb[n].T
    lsi_ = np.ascontiguousarray(
        np.asarray(L0_ffn_lora_start_indices).astype(np.float32))[None, :]

    nc = build_nc()
    in_maps = []
    for b in range(8):
        in_maps.append({
            "x": x[b], "wq": wq_, "wk": wk_, "wv": wv_, "wo": wo_, "wt": wt_,
            "bia": bia_, "lat": lat_, "lbt": lbt_, "lsi": lsi_,
            "cb": np.array([[b]], dtype=np.float32),
            "csq": consts["cs_q"], "csk": consts["cs_k"],
            "tri": consts["tri"], "idn": consts["idn"],
            "e96": consts["e96"], "c3": consts["c3"],
        })
    res = run_bass_kernel_spmd(nc, in_maps, core_ids=list(range(8)))
    return np.stack([np.asarray(r["out"], dtype=np.float32) for r in res.results])


# version 15
# speedup vs baseline: 1.0936x; 1.0936x over previous
import os
from collections import defaultdict
from contextlib import ExitStack

import numpy as np
import ml_dtypes

import concourse.bass as bass
import concourse.mybir as mybir
from concourse.bass_utils import run_bass_kernel_spmd

F32 = mybir.dt.float32
BF16 = mybir.dt.bfloat16
AF = mybir.ActivationFunctionType
ALU = mybir.AluOpType
AX = mybir.AxisListType
BFNP = ml_dtypes.bfloat16

S = 2048
D = 4096
NT = S // 128
NCH = D // 128
H = 16
HD = 256
ROT = 64
EPS = 1e-5
MASKNEG = -3.0e38

ENGINES = ("sync", "scalar", "vector", "tensor", "gpsimd")

DEBUG = os.environ.get("MK_DEBUG", "")
PHASES = os.environ.get("MK_PHASES", "A,B1,B2,B3,C,C2,D,E").split(",")
NHEADS = int(os.environ.get("MK_NHEADS", str(H)))
MK_NT = int(os.environ.get("MK_NT", str(NT)))
A_LVL = int(os.environ.get("MK_ALVL", "9"))


class Res:

    def __init__(self):
        self.w = []
        self.r = []

    def write(self, tok):
        self.w = [tok]
        self.r = []

    def read(self, tok):
        self.r.append(tok)

    def for_read(self):
        return list(self.w)

    def for_write(self):
        return list(self.w) + list(self.r)


class Sched:
    def __init__(self, nc):
        self.nc = nc
        self.q = {e: [] for e in ENGINES}
        self.cnt = defaultdict(int)
        self.sems = {}
        self.pend = {e: [] for e in ENGINES}
        self.active = {e: False for e in ENGINES}

    def sem(self, name):
        if name not in self.sems:
            self.sems[name] = self.nc.alloc_semaphore(name)
        return self.sems[name]

    def do(self, eng, f, waits=(), inc=None, dma=False):
        wd = {}
        for w_ in waits:
            assert w_ is not None, "None token in waits (producer missing inc)"
        for s, v in waits:
            if v > 0:
                wd[s] = max(wd.get(s, 0), v)
        tok = None
        if inc is not None:
            k = 16 if dma else 1
            self.cnt[inc] += k
            tok = (inc, self.cnt[inc])
        rec = {"f": f, "waits": list(self.pend[eng]) + sorted(wd.items()),
               "inc": (inc, 16 if dma else 1) if inc else None}
        self.pend[eng] = []
        self.q[eng].append(rec)
        self.active[eng] = True
        return tok

    def barrier(self):
        for e in ("scalar", "vector", "tensor"):
            if self.active[e]:
                sname = "prog_" + e
                self.cnt[sname] += 1
                v = self.cnt[sname]
                self.q[e].append({"f": None, "waits": [], "inc": None,
                                  "semset": (sname, v)})
        targets = [(s, v) for s, v in self.cnt.items() if v > 0]
        for e in ENGINES:
            self.pend[e] = list(targets)
            self.active[e] = False

    def flush(self, name):
        nc = self.nc
        if not any(self.q[e] for e in ENGINES):
            return
        with nc.Block(name) as blk:
            for e in ENGINES:
                ops = self.q[e]
                if not ops:
                    continue

                def body(engobj, ops=ops):
                    for rec in ops:
                        for s, v in rec["waits"]:
                            engobj.wait_ge(self.sem(s), v)
                        if rec.get("semset"):
                            s, v = rec["semset"]
                            engobj.sem_inc(self.sem(s), 1)
                            continue
                        ins = rec["f"](engobj)
                        if rec["inc"]:
                            s, k = rec["inc"]
                            ins.then_inc(self.sem(s), k)

                getattr(blk, e)(body)
        self.q = {e: [] for e in ENGINES}


def _host_consts():
    inv_freq = (
        1.0
        / np.power(
            np.float32(10000.0),
            np.arange(0, ROT, 2, dtype=np.float32) / np.float32(ROT),
        )
    ).astype(np.float32)
    t = np.arange(S, dtype=np.float32)
    sinu = np.einsum("i,j->ij", t, inv_freq).astype(np.float32)
    sin = np.sin(sinu).astype(np.float32)
    cos = np.cos(sinu).astype(np.float32)
    cos_f = np.repeat(cos.T, 2, axis=0)
    sin_f = np.repeat(sin.T, 2, axis=0)
    cs_q = np.concatenate([cos_f / 16.0, sin_f / 16.0], axis=0).astype(np.float32)
    cs_k = np.concatenate([cos_f, sin_f], axis=0).astype(np.float32)
    tri = np.where(
        np.arange(128)[None, :] <= np.arange(128)[:, None], 0.0, MASKNEG
    ).astype(np.float32)
    idn = np.eye(128, dtype=np.float32)
    e96 = np.zeros((3, 96), dtype=np.float32)
    for n in range(3):
        e96[n, n * 32: n * 32 + 16] = 1.0
    c3 = np.array([[2.0], [3.0], [4.0]], dtype=np.float32)
    return {"cs_q": cs_q, "cs_k": cs_k, "tri": tri, "idn": idn,
            "e96": e96, "c3": c3}


def _ln_tile(sc, nm, x_ap, bn6, mv, rstd, eps_sb, deps, res_x, res_stats,
             lvl=9):
    w_x = res_x.for_read() + deps
    w_s = res_stats.for_write()
    tk = None
    for u in range(8):
        tk = sc.do("vector", lambda e, u=u: e.bn_stats(
            bn6[:, u * 6:(u + 1) * 6], x_ap[:, u * 512:(u + 1) * 512]),
            waits=(w_x + w_s) if u == 0 else [], inc=nm + "_dv")
    res_x.read(tk)
    ag = sc.do("vector", lambda e: e.bn_aggr(mv[:, :], bn6[:, :]),
               waits=[tk], inc=nm + "_dv")
    if lvl < 3:
        return ag, ag
    t3 = sc.do("scalar", lambda e: e.activation(
        rstd[:, :], mv[:, 1:2], AF.Ln, bias=eps_sb[:, 0:1], scale=1.0),
        waits=[ag], inc=nm + "_ac")
    t4 = sc.do("scalar", lambda e: e.activation(
        rstd[:, :], rstd[:, :], AF.Exp, bias=0.0, scale=-0.5),
        waits=[t3], inc=nm + "_ac")
    res_stats.write(t4)
    return ag, t4


def build_nc():
    nc = bass.Bass("TRN2", target_bir_lowering=False)

    x_in = nc.dram_tensor("x", [S, D], F32, kind="ExternalInput").ap()
    wq = nc.dram_tensor("wq", [D, D], F32, kind="ExternalInput").ap()
    wk = nc.dram_tensor("wk", [D, D], F32, kind="ExternalInput").ap()
    wv = nc.dram_tensor("wv", [D, D], F32, kind="ExternalInput").ap()
    wo = nc.dram_tensor("wo", [D, D], F32, kind="ExternalInput").ap()
    wt = nc.dram_tensor("wt", [D, D], F32, kind="ExternalInput").ap()
    bia = nc.dram_tensor("bia", [1, D], F32, kind="ExternalInput").ap()
    lat = nc.dram_tensor("lat", [D, 96], F32, kind="ExternalInput").ap()
    lbt = nc.dram_tensor("lbt", [96, D], F32, kind="ExternalInput").ap()
    lsi = nc.dram_tensor("lsi", [1, 4], F32, kind="ExternalInput").ap()
    cb = nc.dram_tensor("cb", [1, 1], F32, kind="ExternalInput").ap()
    csq = nc.dram_tensor("csq", [128, S], F32, kind="ExternalInput").ap()
    csk = nc.dram_tensor("csk", [128, S], F32, kind="ExternalInput").ap()
    tri = nc.dram_tensor("tri", [128, 128], F32, kind="ExternalInput").ap()
    idn = nc.dram_tensor("idn", [128, 128], F32, kind="ExternalInput").ap()
    e96 = nc.dram_tensor("e96", [3, 96], F32, kind="ExternalInput").ap()
    c3 = nc.dram_tensor("c3", [3, 1], F32, kind="ExternalInput").ap()

    out = nc.dram_tensor("out", [S, D], F32, kind="ExternalOutput").ap()
    qt_d = nc.dram_tensor("qt_d", [D, S], BF16, kind="Internal").ap()
    kt_d = nc.dram_tensor("kt_d", [D, S], BF16, kind="Internal").ap()
    v_d = nc.dram_tensor("v_d", [S, D], BF16, kind="Internal").ap()
    ot_d = nc.dram_tensor("ot_d", [D, S], BF16, kind="Internal").ap()
    attn_d = nc.dram_tensor("attn_d", [S, D], F32, kind="Internal").ap()
    x2_d = nc.dram_tensor("x2_d", [S, D], F32, kind="Internal").ap()
    y_d = nc.dram_tensor("y_d", [S, D], F32, kind="Internal").ap()

    dbg = {}
    for k, shape, dt in (("lnt", [D, S], BF16), ("ot", [D, S], BF16),
                         ("attn", [S, D], F32), ("x2", [S, D], F32),
                         ("y", [S, D], F32), ("m96", [128, 96], BF16),
                         ("rt", [96, S], BF16), ("cc", [1, 8], F32),
                         ("m3", [3, 1], F32)):
        if k in DEBUG:
            dbg[k] = nc.dram_tensor("dbg_" + k, shape, dt,
                                    kind="ExternalOutput").ap()

    sc = Sched(nc)
    gx = ExitStack()
    gbuf = lambda n, s_, d_: gx.enter_context(nc.sbuf_tensor(n, s_, d_))

    idn_sb = gbuf("idn_sb", [128, 128], BF16)
    eps_sb = gbuf("eps_sb", [128, 1], F32)

    t_idn = sc.do("gpsimd", lambda g: g.dma_start(idn_sb[:, :], idn[:, :]),
                  inc="c_idn", dma=True)
    t_eps = sc.do("vector", lambda v: v.memset(eps_sb[:, :], EPS), inc="c_eps")

    def dbg_copy(key, src_ap, rows=None):
        if key in dbg:
            t = sc.do("sync", lambda e, k=key, s_=src_ap, rows=rows:
                      e.dma_start(dbg[k][0:rows, :] if rows else dbg[k][:, :],
                                  s_[0:rows, :] if rows else s_[:, :]),
                      inc="dbgd", dma=True)
            sc.do("sync", lambda e: None, waits=[t])

    ab_stack = ExitStack()
    ln1t = ab_stack.enter_context(nc.sbuf_tensor("ln1t", [128, NCH * 2048], BF16))
    bq_ps = ab_stack.enter_context(nc.psum_tensor("bq_ps", [128, 2048], F32))
    r_bps = [Res() for _ in range(4)]
    a_done = [None] * NT
    if "A" in PHASES:
        with ExitStack() as ph:
            pb = lambda n, s_, d_: ph.enter_context(nc.sbuf_tensor(n, s_, d_))
            x_f = [pb(f"a_x{j}", [128, D], F32) for j in range(2)]
            ln_bf = [pb(f"a_ln{j}", [128, D], BF16) for j in range(2)]
            bn6 = pb("a_bn6", [128, 48], F32)
            mv = pb("a_mv", [128, 2], F32)
            rstd = pb("a_rstd", [128, 1], F32)
            ps_t = [ph.enter_context(nc.psum_tensor(f"a_pst{jj_}", [128, 512], BF16))
                    for jj_ in range(2)]

            r_x = [Res() for _ in range(2)]
            r_ln = [Res() for _ in range(2)]
            r_st = Res()
            r_ps = [Res() for _ in range(2)]

            for i in range(NT):
                j = i % 2
                tk = sc.do("sync", lambda e, j=j, i=i: e.dma_start(
                    x_f[j][:, :], x_in[i * 128:(i + 1) * 128, :]),
                    waits=r_x[j].for_write(), inc=f"a_x{j}", dma=True)
                r_x[j].write(tk)
                _, trs = _ln_tile(sc, "a", x_f[j][:, :], bn6, mv, rstd,
                                  eps_sb, [t_eps], r_x[j], r_st)
                tn = sc.do("vector", lambda e, j=j: e.tensor_scalar(
                    ln_bf[j][:, :], x_f[j][:, :], mv[:, 0:1], rstd[:, 0:1],
                    ALU.subtract, ALU.mult),
                    waits=[trs] + r_ln[j].for_write(), inc="a_dv")
                r_x[j].read(tn)
                r_st.read(tn)
                r_ln[j].write(tn)
                ev = None
                for cg in range(8):
                    half = cg % 2
                    tw = [tn, t_idn] + r_ps[half].for_write()
                    tt = None
                    for cc in range(4):
                        c = cg * 4 + cc
                        tt = sc.do("tensor", lambda e, j=j, c=c, half=half, cc=cc:
                                   e.transpose(
                                       ps_t[half][:, cc * 128:(cc + 1) * 128],
                                       ln_bf[j][:, c * 128:(c + 1) * 128],
                                       idn_sb[:, :]),
                                   waits=tw if cc == 0 else [],
                                   inc="a_pe" if cc == 3 else None)
                    r_ln[j].read(tt)
                    r_ps[half].write(tt)

                    def evac(e, cg=cg, i=i, half=half):
                        dst = ln1t[:, :].rearrange(
                            "p (c t) -> p c t", c=NCH)[
                            :, cg * 4:(cg + 1) * 4, i * 128:(i + 1) * 128]
                        src = ps_t[half][:, :].rearrange(
                            "p (c t) -> p c t", c=4)[:, :, :]
                        return e.activation(dst, src, AF.Copy,
                                            bias=0.0, scale=1.0)
                    ev = sc.do("scalar", evac, waits=[tt], inc="a_ac")
                    r_ps[half].read(ev)
                a_done[i] = ev
            sc.flush("phaseA")

    b1_fin = []
    if "B1" in PHASES:
        with ExitStack() as ph:
            pb = lambda n, s_, d_: ph.enter_context(nc.sbuf_tensor(n, s_, d_))
            w_sb = [pb(f"b1_w{j}", [128, NCH * 128], BF16) for j in range(2)]
            stg = [pb(f"b1_s{j}", [128, 512], BF16) for j in range(2)]
            rot_f = pb("b1_rot", [64, 512], F32)
            sw_f = pb("b1_sw", [64, 512], F32)
            tmp_f = pb("b1_tmp", [64, 512], F32)
            cskc = pb("b1_cc", [64, S], F32)
            csks = pb("b1_cs", [64, S], F32)

            for t_, r0 in ((cskc, 0), (csks, 64)):
                sc.do("sync", lambda e, t_=t_, r0=r0: e.dma_start(
                    t_[:, :], csk[r0:r0 + 64, :]), inc="b1_tab", dma=True)
            t_tab1 = ("b1_tab", sc.cnt["b1_tab"])

            r_w = [Res() for _ in range(2)]
            r_stg = [Res() for _ in range(2)]
            r_rot, r_sw, r_tmp = Res(), Res(), Res()
            tmz = None
            for t_ in (rot_f, sw_f, tmp_f):
                tmz = sc.do("vector", lambda e, t_=t_: e.memset(t_[:, :], 0.0),
                            inc="b1_dv")
            r_rot.write(tmz)
            r_sw.write(tmz)
            r_tmp.write(tmz)

            blocks = ([("q", ob) for ob in range(32)]
                      + [("k", ob) for ob in range(32)])

            def load_w1(idx, waits):
                m, ob = blocks[idx]
                src = wq if m == "q" else wk
                j = idx % 2
                tok = None
                for hh in range(2):
                    def f(e, src=src, ob=ob, j=j, hh=hh):
                        s_ = src.rearrange("(c p) o -> p c o", p=128)[
                            :, hh * 16:(hh + 1) * 16, ob * 128:(ob + 1) * 128]
                        d_ = w_sb[j][:, :].rearrange(
                            "p (c o) -> p c o", c=NCH)[
                            :, hh * 16:(hh + 1) * 16, :]
                        return e.dma_start(d_, s_)
                    tok = sc.do("gpsimd", f, waits=waits if hh == 0 else [],
                                inc=f"b1_w{j}", dma=True)
                return tok

            t0 = load_w1(0, [])
            r_w[0].write(t0)
            gctr = [0]
            for idx in range(64):
                m, ob = blocks[idx]
                j = idx % 2
                sca = (1.0 / 16.0) if m == "q" else 1.0
                dst_d = qt_d if m == "q" else kt_d
                if idx + 1 < 64:
                    jn = (idx + 1) % 2
                    tn_ = load_w1(idx + 1, r_w[jn].for_write())
                    r_w[jn].write(tn_)
                for tt4 in range(4):
                    reg = gctr[0] % 4
                    sj = gctr[0] % 2
                    gctr[0] += 1
                    w_ = (r_w[j].for_read() + r_bps[reg].for_write()
                          + ([a_done[tt4 * 4 + 3]]
                             if a_done[tt4 * 4 + 3] is not None else []))
                    tmm = None
                    for c in range(NCH):
                        def f(e, j=j, c=c, tt4=tt4, reg=reg):
                            return e.matmul(
                                bq_ps[:, reg * 512:(reg + 1) * 512],
                                w_sb[j][:, :].rearrange(
                                    "p (c o) -> p c o", c=NCH)[:, c, :],
                                ln1t[:, :].rearrange(
                                    "p (c t) -> p c t", c=NCH)[
                                    :, c, tt4 * 512:(tt4 + 1) * 512],
                                start=(c == 0), stop=(c == NCH - 1))
                        tmm = sc.do("tensor", f, waits=w_ if c == 0 else [],
                                    inc="b1_pe" if c == NCH - 1 else None)
                    r_w[j].read(tmm)
                    r_bps[reg].write(tmm)
                    te = sc.do("scalar", lambda e, reg=reg, sj=sj, sca=sca:
                               e.activation(
                                   stg[sj][:, :],
                                   bq_ps[:, reg * 512:(reg + 1) * 512],
                                   AF.Copy, bias=0.0, scale=sca),
                               waits=[tmm] + r_stg[sj].for_write(),
                               inc="b1_ac")
                    r_bps[reg].read(te)
                    r_stg[sj].write(te)
                    dma_wait = [te]
                    if ob % 2 == 0:
                        tr_ = sc.do("scalar", lambda e, reg=reg, sca=sca:
                                    e.activation(
                                        rot_f[0:64, :],
                                        bq_ps[0:64, reg * 512:reg * 512 + 512],
                                        AF.Copy, bias=0.0, scale=sca),
                                    waits=[tmm] + r_rot.for_write(),
                                    inc="b1_ac")
                        r_rot.write(tr_)
                        r_bps[reg].read(tr_)
                        swmask = [i ^ 1 for i in range(32)]
                        swt = sc.do("vector", lambda e, swmask=swmask:
                                    e.stream_shuffle(
                                        sw_f[0:64, :], rot_f[0:64, :], swmask),
                                    waits=[tr_] + r_sw.for_write(),
                                    inc="b1_dv")
                        r_sw.write(swt)
                        r_rot.read(swt)
                        t1 = sc.do("vector", lambda e, tt4=tt4: e.tensor_mul(
                            tmp_f[0:64, :], rot_f[0:64, :],
                            cskc[0:64, tt4 * 512:(tt4 + 1) * 512]),
                            waits=[tr_, t_tab1] + r_tmp.for_write(),
                            inc="b1_dv")
                        r_tmp.write(t1)
                        r_rot.read(t1)
                        t2 = sc.do("vector", lambda e, tt4=tt4: e.tensor_mul(
                            sw_f[0:64, :], sw_f[0:64, :],
                            csks[0:64, tt4 * 512:(tt4 + 1) * 512]),
                            waits=[swt, t_tab1], inc="b1_dv")
                        r_sw.write(t2)
                        t3 = sc.do("vector", lambda e, sj=sj: e.tensor_add(
                            stg[sj][0:64, :], tmp_f[0:64, :], sw_f[0:64, :]),
                            waits=[t1, t2, te], inc="b1_dv")
                        r_tmp.read(t3)
                        r_sw.read(t3)
                        r_stg[sj].w.append(t3)
                        dma_wait = [te, t3]
                    ts_ = sc.do("sync", lambda e, sj=sj, dst_d=dst_d,
                                ob=ob, tt4=tt4: e.dma_start(
                                    dst_d[ob * 128:(ob + 1) * 128,
                                          tt4 * 512:(tt4 + 1) * 512],
                                    stg[sj][:, :]),
                                waits=dma_wait, inc=f"b1_st{reg}", dma=True)
                    r_stg[sj].read(ts_)
            sc.flush("phaseB1")
        b1_fin = [("b1_st0", sc.cnt["b1_st0"]), ("b1_st1", sc.cnt["b1_st1"])]

    if "B2" in PHASES:
        with ExitStack() as ph:
            pb = lambda n, s_, d_: ph.enter_context(nc.sbuf_tensor(n, s_, d_))
            wv_sb = [pb(f"b2_w{j}", [128, NCH * 512], BF16) for j in range(2)]
            vst = [pb(f"b2_s{j}", [128, 512], BF16) for j in range(2)]
            r_wv = [Res() for _ in range(2)]
            r_vst = [Res() for _ in range(2)]
            alias_w = list(b1_fin) + ([a_done[NT - 1]]
                                      if a_done[NT - 1] is not None else [])

            def load_wv(ob, j, waits):
                tok = None
                for hh in range(2):
                    def f(e, ob=ob, j=j, hh=hh):
                        src = wv.rearrange("(c p) o -> p c o", p=128)[
                            :, hh * 16:(hh + 1) * 16,
                            ob * 512:(ob + 1) * 512]
                        dst = wv_sb[j][:, :].rearrange(
                            "p (c o) -> p c o", c=NCH)[
                            :, hh * 16:(hh + 1) * 16, :]
                        return e.dma_start(dst, src)
                    tok = sc.do("gpsimd", f, waits=waits if hh == 0 else [],
                                inc=f"b2_w{j}", dma=True)
                return tok

            t0 = load_wv(0, 0, alias_w)
            r_wv[0].write(t0)
            gctr2 = [0]
            for ob in range(8):
                j = ob % 2
                if ob + 1 < 8:
                    jn = (ob + 1) % 2
                    tn_ = load_wv(ob + 1, jn, r_wv[jn].for_write()
                                  + (alias_w if ob == 0 else []))
                    r_wv[jn].write(tn_)
                for i in range(NT):
                    reg = gctr2[0] % 4
                    vj = i % 2
                    gctr2[0] += 1
                    w_ = (r_wv[j].for_read() + r_bps[reg].for_write()
                          + ([a_done[i]] if a_done[i] is not None else []))
                    tmm = None
                    for c in range(NCH):
                        def f(e, c=c, i=i, j=j, reg=reg):
                            return e.matmul(
                                bq_ps[:, reg * 512:(reg + 1) * 512],
                                ln1t[:, :].rearrange(
                                    "p (c t) -> p c t", c=NCH)[
                                    :, c, i * 128:(i + 1) * 128],
                                wv_sb[j][:, :].rearrange(
                                    "p (c o) -> p c o", c=NCH)[:, c, :],
                                start=(c == 0), stop=(c == NCH - 1))
                        tmm = sc.do("tensor", f, waits=w_ if c == 0 else [],
                                    inc="b2_pe" if c == NCH - 1 else None)
                    r_wv[j].read(tmm)
                    r_bps[reg].write(tmm)
                    te = sc.do("scalar", lambda e, reg=reg, vj=vj: e.activation(
                        vst[vj][:, :], bq_ps[:, reg * 512:(reg + 1) * 512],
                        AF.Copy, bias=0.0, scale=1.0),
                        waits=[tmm] + r_vst[vj].for_write()
                        + (alias_w if (ob == 0 and i == 0) else []),
                        inc="b2_ac")
                    r_bps[reg].read(te)
                    r_vst[vj].write(te)
                    ts_ = sc.do("sync", lambda e, vj=vj, i=i, ob=ob:
                                e.dma_start(
                                    v_d[i * 128:(i + 1) * 128,
                                        ob * 512:(ob + 1) * 512],
                                    vst[vj][:, :]),
                                waits=[te], inc=f"b2_st{reg}", dma=True)
                    r_vst[vj].read(ts_)
            sc.flush("phaseB2")

    sc.barrier()
    ab_stack.close()

    if "B3" in PHASES:
        with ExitStack() as ph:
            pb = lambda n, s_, d_: ph.enter_context(nc.sbuf_tensor(n, s_, d_))
            qT = [[pb(f"b3_q{u}{b}", [128, 2 * S], BF16) for b in range(2)]
                  for u in range(2)]
            kT = [[pb(f"b3_k{u}{b}", [128, 2 * S], BF16) for b in range(2)]
                  for u in range(2)]
            v_sb = [[pb(f"b3_v{u}{b}", [128, NT * 256], BF16)
                     for b in range(2)] for u in range(2)]
            P_sb = [pb(f"b3_P{u}", [128, 2 * S], BF16) for u in range(2)]
            PT_sb = [pb(f"b3_PT{u}", [128, 1024], BF16) for u in range(2)]
            O_acc = [pb(f"b3_O{u}", [128, NT * HD], BF16) for u in range(2)]
            OT_sb = [pb(f"b3_OT{u}", [128, 2 * S], BF16) for u in range(2)]
            mx = pb("b3_mx", [128, 8], F32)
            mneg2 = pb("b3_mneg", [128, 4], F32)
            lc = pb("b3_lc", [128, 16], F32)
            rl2 = pb("b3_rl", [128, 4], F32)
            tri_sb = pb("b3_tri", [128, 128], F32)
            ps_s = ph.enter_context(nc.psum_tensor("b3_pss", [128, 2048], F32))
            ps_o = [ph.enter_context(nc.psum_tensor(f"b3_pso{u}", [128, 512],
                                                    F32)) for u in range(2)]
            ps_p = [ph.enter_context(nc.psum_tensor(f"b3_psp{u}", [128, 512],
                                                    BF16)) for u in range(2)]

            sc.do("sync", lambda e: e.dma_start(tri_sb[:, :], tri[:, :]),
                  inc="b3_tab", dma=True)
            t_tab = ("b3_tab", sc.cnt["b3_tab"])

            r_q = [[Res() for _ in range(2)] for _ in range(2)]
            r_k = [[Res() for _ in range(2)] for _ in range(2)]
            r_v = [[Res() for _ in range(2)] for _ in range(2)]
            r_P = [[Res() for _ in range(2)] for _ in range(2)]
            r_PT = [[Res() for _ in range(2)] for _ in range(2)]
            r_Oacc = [Res() for _ in range(2)]
            r_OT = [Res() for _ in range(2)]
            r_mx = [Res() for _ in range(2)]
            r_lc = [[Res() for _ in range(2)] for _ in range(2)]
            r_mneg = [[Res() for _ in range(2)] for _ in range(2)]
            r_rl = [[Res() for _ in range(2)] for _ in range(2)]
            r_sreg = [Res() for _ in range(4)]
            r_pso = [Res() for _ in range(2)]
            r_psp = [Res() for _ in range(2)]
            s_ctr = [0]

            def load_pair(g):
                b = g % 2
                sem = f"b3_l{b}"
                for u in range(2):
                    h = 2 * g + u
                    wfull = (r_q[u][b].for_write() + r_k[u][b].for_write()
                             + r_v[u][b].for_write())
                    sc.do("sync", lambda e, h=h, u=u, b=b: e.dma_start(
                        qT[u][b][:, 0:S], qt_d[h * HD:h * HD + 128, :]),
                        waits=wfull, inc=sem, dma=True)
                    sc.do("sync", lambda e, h=h, u=u, b=b: e.dma_start(
                        qT[u][b][:, S:2 * S],
                        qt_d[h * HD + 128:h * HD + 256, :]),
                        inc=sem, dma=True)
                    sc.do("gpsimd", lambda e, h=h, u=u, b=b: e.dma_start(
                        kT[u][b][:, 0:S], kt_d[h * HD:h * HD + 128, :]),
                        waits=wfull, inc=sem, dma=True)
                    sc.do("gpsimd", lambda e, h=h, u=u, b=b: e.dma_start(
                        kT[u][b][:, S:2 * S],
                        kt_d[h * HD + 128:h * HD + 256, :]),
                        inc=sem, dma=True)

                    def fv(e, h=h, u=u, b=b):
                        src = v_d.rearrange("(t p) c -> p t c", p=128)[
                            :, :, h * HD:(h + 1) * HD]
                        dst = v_sb[u][b][:, :].rearrange(
                            "p (t c) -> p t c", t=NT)
                        return e.dma_start(dst, src)
                    sc.do("sync", fv, inc=sem, dma=True)
                tok = (sem, sc.cnt[sem])
                for u in range(2):
                    r_q[u][b].write(tok)
                    r_k[u][b].write(tok)
                    r_v[u][b].write(tok)

            def emit_scores(u, b, qi):
                rowlen = (qi + 1) * 128
                nchk = (rowlen + 511) // 512
                chks = []
                ldtok = r_q[u][b].for_read() + r_k[u][b].for_read()
                for ci in range(nchk):
                    k0 = ci * 512
                    klen = min(rowlen - k0, 512)
                    reg = s_ctr[0] % 4
                    s_ctr[0] += 1
                    off = reg * 512
                    chks.append((ci, k0, klen, off, reg))
                    w_ = r_sreg[reg].for_write() + ldtok
                    tt = None
                    for st in range(2):
                        def f(e, u=u, b=b, qi=qi, k0=k0, klen=klen, st=st,
                              off=off):
                            return e.matmul(
                                ps_s[:, off:off + klen],
                                qT[u][b][:, st * S + qi * 128:
                                         st * S + (qi + 1) * 128],
                                kT[u][b][:, st * S + k0:st * S + k0 + klen],
                                start=(st == 0), stop=(st == 1))
                        tt = sc.do("tensor", f, waits=w_ if st == 0 else [],
                                   inc="b3_pe" if st == 1 else None)
                    r_q[u][b].read(tt)
                    r_k[u][b].read(tt)
                    r_sreg[reg].write(tt)
                    if k0 + klen == rowlen:
                        dlo = off + klen - 128
                        tt = sc.do("vector", lambda e, dlo=dlo: e.tensor_add(
                            ps_s[:, dlo:dlo + 128], ps_s[:, dlo:dlo + 128],
                            tri_sb[:, :]),
                            waits=[tt, t_tab], inc="b3_dv")
                        r_sreg[reg].w.append(tt)
                    tm = sc.do("vector", lambda e, u=u, off=off, klen=klen,
                               ci=ci: e.reduce_max(
                                   mx[:, u * 4 + ci:u * 4 + ci + 1],
                                   ps_s[:, off:off + klen], AX.X),
                               waits=[tt] + (r_mx[u].for_write()
                                             if ci == 0 else []),
                               inc="b3_dv")
                    r_sreg[reg].read(tm)
                    if ci == 0:
                        r_mx[u].write(tm)
                    else:
                        r_mx[u].w.append(tm)
                pj = qi % 2
                tmn = sc.do("vector", lambda e, u=u, nchk=nchk, pj=pj:
                            e.reduce_max(
                                mneg2[:, u * 2 + pj:u * 2 + pj + 1],
                                mx[:, u * 4:u * 4 + nchk], AX.X, negate=True),
                            waits=r_mx[u].w + r_mneg[u][pj].for_write(),
                            inc="b3_dv")
                r_mx[u].read(tmn)
                r_mneg[u][pj].write(tmn)
                return {"u": u, "b": b, "qi": qi, "pj": pj, "tmn": tmn,
                        "chks": chks}

            def emit_softmax(st_):
                u, qi, pj, tmn, chks = (st_["u"], st_["qi"], st_["pj"],
                                        st_["tmn"], st_["chks"])
                nchk = len(chks)
                etoks = []
                for (ci, k0, klen, off, reg) in chks:
                    def fe(e, u=u, off=off, klen=klen, k0=k0, ci=ci, pj=pj):
                        return e.activation(
                            P_sb[u][:, pj * S + k0:pj * S + k0 + klen],
                            ps_s[:, off:off + klen], AF.Exp,
                            bias=mneg2[:, u * 2 + pj:u * 2 + pj + 1],
                            scale=1.0,
                            accum_out=lc[:, u * 8 + pj * 4 + ci:
                                         u * 8 + pj * 4 + ci + 1])
                    te = sc.do("scalar", fe,
                               waits=[tmn] + (r_P[u][pj].for_write()
                                              + r_lc[u][pj].for_write()
                                              if ci == 0 else []),
                               inc="b3_ac")
                    etoks.append(te)
                    r_sreg[reg].read(te)
                    r_mneg[u][pj].read(te)
                    if ci == 0:
                        r_P[u][pj].write(te)
                        r_lc[u][pj].write(te)
                    else:
                        r_P[u][pj].w.append(te)
                        r_lc[u][pj].w.append(te)

                tl = sc.do("vector", lambda e, u=u, nchk=nchk, pj=pj:
                           e.reduce_sum(
                               rl2[:, u * 2 + pj:u * 2 + pj + 1],
                               lc[:, u * 8 + pj * 4:u * 8 + pj * 4 + nchk],
                               AX.X),
                           waits=etoks + r_rl[u][pj].for_write(),
                           inc="b3_dv")
                r_lc[u][pj].read(tl)
                trl = sc.do("vector", lambda e, u=u, pj=pj: e.reciprocal(
                    rl2[:, u * 2 + pj:u * 2 + pj + 1],
                    rl2[:, u * 2 + pj:u * 2 + pj + 1]),
                    waits=[tl], inc="b3_dv")
                r_rl[u][pj].write(trl)
                st_["etoks"] = etoks
                st_["trl"] = trl
                return st_

            def emit_pv(st_):
                u, b, qi, pj = st_["u"], st_["b"], st_["qi"], st_["pj"]
                etoks = st_["etoks"]
                trl = st_["trl"]
                ngrp = (qi + 4) // 4
                pv_tok = None
                for gp in range(ngrp):
                    jlo = gp * 4
                    jhi = min(qi + 1, jlo + 4)
                    pbh = gp % 2
                    tt = None
                    for jj_ in range(jlo, jhi):
                        def ft(e, u=u, pj=pj, jj_=jj_):
                            return e.transpose(
                                ps_p[u][:, (jj_ % 4) * 128:
                                       (jj_ % 4 + 1) * 128],
                                P_sb[u][:, pj * S + jj_ * 128:
                                        pj * S + (jj_ + 1) * 128],
                                idn_sb[:, :])
                        tt = sc.do("tensor", ft,
                                   waits=([etoks[gp], t_idn]
                                          + r_psp[u].for_write())
                                   if jj_ == jlo else [],
                                   inc="b3_pe" if jj_ == jhi - 1 else None)
                    r_P[u][pj].read(tt)
                    r_psp[u].write(tt)

                    def fev(e, u=u, pbh=pbh, n=(jhi - jlo)):
                        return e.activation(
                            PT_sb[u][:, pbh * 512:pbh * 512 + n * 128],
                            ps_p[u][:, 0:n * 128], AF.Copy,
                            bias=0.0, scale=1.0)
                    tev = sc.do("scalar", fev,
                                waits=[tt] + r_PT[u][pbh].for_write(),
                                inc="b3_ac")
                    r_psp[u].read(tev)
                    r_PT[u][pbh].write(tev)
                    for jj_ in range(jlo, jhi):
                        def fpv(e, u=u, b=b, pbh=pbh, jj_=jj_, qi=qi):
                            return e.matmul(
                                ps_o[u][:, 0:HD],
                                PT_sb[u][:, pbh * 512 + (jj_ % 4) * 128:
                                        pbh * 512 + (jj_ % 4 + 1) * 128],
                                v_sb[u][b][:, jj_ * 256:(jj_ + 1) * 256],
                                start=(jj_ == 0), stop=(jj_ == qi),
                                skip_group_check=True)
                        ww = [tev] + r_v[u][b].for_read()
                        if jj_ == 0:
                            ww = ww + r_pso[u].for_write()
                        pv_tok = sc.do("tensor", fpv,
                                       waits=ww if jj_ in (jlo, 0) else [],
                                       inc="b3_pe" if jj_ == jhi - 1
                                       else None)
                    r_PT[u][pbh].read(pv_tok)
                r_v[u][b].read(pv_tok)
                r_pso[u].write(pv_tok)

                tof = sc.do("vector", lambda e, u=u, pj=pj, qi=qi:
                            e.tensor_scalar(
                                O_acc[u][:, qi * HD:(qi + 1) * HD],
                                ps_o[u][:, 0:HD],
                                rl2[:, u * 2 + pj:u * 2 + pj + 1], None,
                                ALU.mult),
                            waits=[pv_tok, trl]
                            + (r_Oacc[u].for_write() if qi == 0 else []),
                            inc="b3_dv")
                r_rl[u][pj].read(tof)
                r_pso[u].read(tof)
                if qi == 0:
                    r_Oacc[u].write(tof)
                else:
                    r_Oacc[u].w.append(tof)

            load_pair(0)
            for g in range(8):
                b = g % 2
                prevA = prevB = None
                for qi in range(NT):
                    curA = emit_scores(0, b, qi)
                    if qi == 1 and g + 1 < 8:
                        load_pair(g + 1)
                    if prevB is not None:
                        emit_pv(prevB)
                    curA = emit_softmax(curA)
                    curB = emit_scores(1, b, qi)
                    if prevA is not None:
                        emit_pv(prevA)
                    curB = emit_softmax(curB)
                    prevA, prevB = curA, curB
                emit_pv(prevA)
                emit_pv(prevB)
                for u in range(2):
                    for st in range(2):
                        for grp in range(4):
                            tt = None
                            for k in range(4):
                                qi_ = grp * 4 + k

                                def ftr(e, u=u, st=st, qi_=qi_, k=k):
                                    return e.transpose(
                                        ps_p[u][:, k * 128:(k + 1) * 128],
                                        O_acc[u][:, qi_ * HD + st * 128:
                                               qi_ * HD + (st + 1) * 128],
                                        idn_sb[:, :])
                                tt = sc.do("tensor", ftr,
                                           waits=([r_Oacc[u].w[-1], t_idn]
                                                  + r_psp[u].for_write())
                                           if k == 0 else [],
                                           inc="b3_pe" if k == 3 else None)
                            r_Oacc[u].read(tt)
                            r_psp[u].write(tt)

                            def fevo(e, u=u, st=st, grp=grp):
                                return e.tensor_copy(
                                    OT_sb[u][:, st * S + grp * 512:
                                           st * S + (grp + 1) * 512],
                                    ps_p[u][:, 0:512])
                            tev = sc.do("vector", fevo,
                                        waits=[tt] + (r_OT[u].for_write()
                                                      if (st == 0
                                                          and grp == 0)
                                                      else []),
                                        inc="b3_dv")
                            r_psp[u].read(tev)
                            if st == 0 and grp == 0:
                                r_OT[u].write(tev)
                            else:
                                r_OT[u].w.append(tev)
                for u in range(2):
                    h = 2 * g + u
                    st_tok = None
                    for s_ in range(2):
                        def fst(e, u=u, s_=s_, h=h):
                            return e.dma_start(
                                ot_d[h * HD + s_ * 128:
                                     h * HD + (s_ + 1) * 128, :],
                                OT_sb[u][:, s_ * S:(s_ + 1) * S])
                        st_tok = sc.do("sync", fst,
                                       waits=r_OT[u].for_write()
                                       if s_ == 0 else [],
                                       inc="b3_otd", dma=True)
                    r_OT[u].read(st_tok)

            sc.barrier()
            dbg_copy("ot", ot_d)
            sc.barrier()
            sc.flush("phaseB3")


    if "C" in PHASES:
        with ExitStack() as ph:
            pb = lambda n, s_, d_: ph.enter_context(nc.sbuf_tensor(n, s_, d_))
            ot_sb = pb("c_ot", [128, NCH * 2048], BF16)
            wo_sb = [pb(f"c_wo{j}", [128, NCH * 512], BF16) for j in range(2)]
            at_sb = [pb(f"c_at{j}", [128, 512], F32) for j in range(2)]
            ps_c = ph.enter_context(nc.psum_tensor("c_ps", [128, 1024], F32))

            r_ot, r_wo = Res(), [Res() for _ in range(2)]
            r_at = [Res() for _ in range(2)]
            r_psc = [Res() for _ in range(2)]

            tokh = None
            tok = None
            for th in range(2):
                for qd in range(4):
                    def f(e, qd=qd, th=th):
                        src = ot_d.rearrange("(c p) t -> p c t", p=128)[
                            :, qd * 8:(qd + 1) * 8,
                            th * 1024:(th + 1) * 1024]
                        dst = ot_sb[:, :].rearrange(
                            "p (c t) -> p c t", c=NCH)[
                            :, qd * 8:(qd + 1) * 8,
                            th * 1024:(th + 1) * 1024]
                        return e.dma_start(dst, src)
                    tok = sc.do("sync" if qd % 2 == 0 else "gpsimd", f,
                                inc="c_otl", dma=True)
                if th == 0:
                    tokh = ("c_otl", sc.cnt["c_otl"])
            r_ot.write(tok)
            r_oth = Res()
            r_oth.write(tokh)

            def load_wo(ob, j, waits):
                tok = None
                for hh in range(2):
                    def f(e, ob=ob, j=j, hh=hh):
                        src = wo.rearrange("(c p) o -> p c o", p=128)[
                            :, hh * 16:(hh + 1) * 16,
                            ob * 512:(ob + 1) * 512]
                        dst = wo_sb[j][:, :].rearrange(
                            "p (c o) -> p c o", c=NCH)[
                            :, hh * 16:(hh + 1) * 16, :]
                        return e.dma_start(dst, src)
                    tok = sc.do("gpsimd", f, waits=waits if hh == 0 else [],
                                inc=f"c_wo{j}d", dma=True)
                return tok

            t0 = load_wo(0, 0, [])
            r_wo[0].write(t0)
            for ob in range(8):
                j = ob % 2
                if ob + 1 < 8:
                    jn = (ob + 1) % 2
                    tn_ = load_wo(ob + 1, jn, r_wo[jn].for_write())
                    r_wo[jn].write(tn_)
                for i in range(NT):
                    reg = i % 2
                    w_ = (r_wo[j].for_read()
                          + (r_oth.for_read() if i < 8 else r_ot.for_read())
                          + r_psc[reg].for_write())
                    tt = None
                    for c in range(NCH):
                        def f(e, c=c, i=i, j=j, reg=reg):
                            return e.matmul(
                                ps_c[:, reg * 512:reg * 512 + 512],
                                ot_sb[:, :].rearrange(
                                    "p (c t) -> p c t", c=NCH)[
                                    :, c, i * 128:(i + 1) * 128],
                                wo_sb[j][:, :].rearrange(
                                    "p (c o) -> p c o", c=NCH)[:, c, :],
                                start=(c == 0), stop=(c == NCH - 1))
                        tt = sc.do("tensor", f, waits=w_ if c == 0 else [],
                                   inc="c_pe" if c == NCH - 1 else None)
                    r_wo[j].read(tt)
                    r_ot.read(tt)
                    r_psc[reg].write(tt)

                    def ev(e, reg=reg):
                        return e.activation(
                            at_sb[reg][:, :], ps_c[:, reg * 512:reg * 512 + 512],
                            AF.Copy, bias=0.0, scale=1.0)
                    te = sc.do("scalar", ev,
                               waits=[tt] + r_at[reg].for_write(), inc="c_ac")
                    r_psc[reg].read(te)
                    r_at[reg].write(te)

                    def fst(e, reg=reg, i=i, ob=ob):
                        return e.dma_start(
                            attn_d[i * 128:(i + 1) * 128,
                                   ob * 512:(ob + 1) * 512],
                            at_sb[reg][:, :])
                    ts_ = sc.do("sync", fst, waits=[te], inc=f"c_std{reg}",
                                dma=True)
                    r_at[reg].read(ts_)

            sc.barrier()
            dbg_copy("attn", attn_d)
            sc.barrier()
            sc.flush("phaseC")

    ln3_stack = ExitStack()
    if "C2" in PHASES:
        ln3t = ln3_stack.enter_context(
            nc.sbuf_tensor("d_ln3t", [128, NCH * 2048], BF16))
        r_ln3t = Res()
        with ExitStack() as ph:
            pb = lambda n, s_, d_: ph.enter_context(nc.sbuf_tensor(n, s_, d_))
            x_f = [pb(f"g_x{j}", [128, D], F32) for j in range(2)]
            a_f = [pb(f"g_a{j}", [128, D], F32) for j in range(2)]
            ln_bf = pb("g_ln", [128, D], BF16)
            bn6 = pb("g_bn6", [128, 48], F32)
            mv = pb("g_mv", [128, 2], F32)
            rstd = pb("g_rstd", [128, 1], F32)
            ps_t = [ph.enter_context(nc.psum_tensor(f"g_pst{jj_}", [128, 512], BF16))
                    for jj_ in range(2)]

            r_x = [Res() for _ in range(2)]
            r_a = [Res() for _ in range(2)]
            r_ln = Res()
            r_st = Res()
            r_ps = [Res() for _ in range(2)]

            for i in range(NT):
                j = i % 2
                tx = sc.do("sync", lambda e, j=j, i=i: e.dma_start(
                    x_f[j][:, :], x_in[i * 128:(i + 1) * 128, :]),
                    waits=r_x[j].for_write(), inc=f"g_x{j}", dma=True)
                r_x[j].write(tx)
                ta = sc.do("sync", lambda e, j=j, i=i: e.dma_start(
                    a_f[j][:, :], attn_d[i * 128:(i + 1) * 128, :]),
                    waits=r_a[j].for_write(), inc=f"g_a{j}", dma=True)
                r_a[j].write(ta)

                _, trs = _ln_tile(sc, "g", a_f[j][:, :], bn6, mv, rstd,
                                  eps_sb, [t_eps], r_a[j], r_st)
                tn = sc.do("vector", lambda e, j=j: e.tensor_scalar(
                    a_f[j][:, :], a_f[j][:, :], mv[:, 0:1], rstd[:, 0:1],
                    ALU.subtract, ALU.mult), waits=[trs], inc="g_dv")
                r_a[j].write(tn)
                r_st.read(tn)
                tn2 = sc.do("vector", lambda e, j=j: e.tensor_add(
                    x_f[j][:, :], x_f[j][:, :], a_f[j][:, :]),
                    waits=[tn, tx], inc="g_dv")
                r_x[j].write(tn2)
                r_a[j].read(tn2)
                ts_ = sc.do("sync", lambda e, j=j, i=i: e.dma_start(
                    x2_d[i * 128:(i + 1) * 128, :], x_f[j][:, :]),
                    waits=[tn2], inc=f"g_x2d{j}", dma=True)
                r_x[j].read(ts_)
                _, trs3 = _ln_tile(sc, "g3", x_f[j][:, :], bn6, mv, rstd,
                                   eps_sb, [], r_x[j], r_st)
                tn3 = sc.do("vector", lambda e, j=j: e.tensor_scalar(
                    ln_bf[:, :], x_f[j][:, :], mv[:, 0:1], rstd[:, 0:1],
                    ALU.subtract, ALU.mult),
                    waits=[trs3] + r_ln.for_write(), inc="g_dv")
                r_x[j].read(tn3)
                r_st.read(tn3)
                r_ln.write(tn3)
                for cg in range(8):
                    half = cg % 2
                    tw = [tn3, t_idn] + r_ps[half].for_write()
                    tt = None
                    for cc in range(4):
                        c = cg * 4 + cc
                        tt = sc.do("tensor", lambda e, c=c, half=half, cc=cc:
                                   e.transpose(
                                       ps_t[half][:, cc * 128:(cc + 1) * 128],
                                       ln_bf[:, c * 128:(c + 1) * 128],
                                       idn_sb[:, :]),
                                   waits=tw if cc == 0 else [],
                                   inc="g_pe" if cc == 3 else None)
                    r_ln.read(tt)
                    r_ps[half].write(tt)

                    def evac(e, cg=cg, i=i, half=half):
                        dst = ln3t[:, :].rearrange(
                            "p (c t) -> p c t", c=NCH)[
                            :, cg * 4:(cg + 1) * 4, i * 128:(i + 1) * 128]
                        src = ps_t[half][:, :].rearrange(
                            "p (c t) -> p c t", c=4)[:, :, :]
                        return e.activation(dst, src, AF.Copy,
                                            bias=0.0, scale=1.0)
                    ev = sc.do("scalar", evac, waits=[tt], inc="g_ac")
                    r_ps[half].read(ev)
                    r_ln3t.w.append(ev)

            sc.barrier()
            dbg_copy("x2", x2_d)
            sc.barrier()
            sc.flush("phaseC2")

    if "D" in PHASES:
        with ExitStack() as ph:
            pb = lambda n, s_, d_: ph.enter_context(nc.sbuf_tensor(n, s_, d_))
            rt_sb = pb("d_rt", [96, S], BF16)
            bia_s = pb("d_bias", [1, 512], F32)
            ones1 = pb("d_ones1", [1, 128], F32)
            bb_sb = pb("d_bb", [128, 512], F32)
            bts = [pb(f"d_bt{j}", [96, 512], BF16) for j in range(2)]
            y_sb = [pb(f"d_y{j}", [128, 512], F32) for j in range(2)]
            ps_d = ph.enter_context(nc.psum_tensor("d_ps", [128, 2048], F32))
            ps_s = ph.enter_context(nc.psum_tensor("d_pss", [128, 512], F32))
            ps_r = ph.enter_context(nc.psum_tensor("d_psr", [128, 128], BF16))
            ph1 = ExitStack()
            pb1 = lambda n, s_, d_: ph1.enter_context(nc.sbuf_tensor(n, s_, d_))
            lsi_sb = pb1("d_lsi", [1, 4], F32)
            cb_sb = pb1("d_cb", [1, 1], F32)
            cc_sb = pb1("d_cc", [1, 8], F32)
            ones3 = pb1("d_ones3", [3, 128], F32)
            c3_sb = pb1("d_c3", [3, 1], F32)
            m3 = pb1("d_m3", [3, 1], F32)
            e96_sb = pb1("d_e96", [3, 96], F32)
            me_sb = pb1("d_me", [3, 96], F32)
            m96 = pb1("d_m96", [128, 96], BF16)
            at_sb = pb1("d_at", [128, NCH * 96], BF16)
            rtmp = pb1("d_rtmp", [128, 96], BF16)

            tl1 = sc.do("sync", lambda e: e.dma_start(lsi_sb[:, :], lsi[:, :]),
                        inc="d_cl", dma=True)
            tl2 = sc.do("sync", lambda e: e.dma_start(cb_sb[:, :], cb[:, :]),
                        inc="d_cl", dma=True)
            tl3 = sc.do("sync", lambda e: e.dma_start(c3_sb[:, :], c3[:, :]),
                        inc="d_cl", dma=True)
            tl4 = sc.do("sync", lambda e: e.dma_start(e96_sb[:, :], e96[:, :]),
                        inc="d_cl", dma=True)
            tcl = ("d_cl", sc.cnt["d_cl"])
            tl1 = tl2 = tl3 = tl4 = tcl

            tat = sc.do("gpsimd", lambda e: e.dma_start(
                at_sb[:, :].rearrange("p (c j) -> p c j", c=NCH),
                lat.rearrange("(c p) j -> p c j", p=128)),
                inc="d_atl", dma=True)

            to1 = sc.do("vector", lambda e: e.memset(ones3[:, :], 1.0),
                        inc="d_dv")
            to2 = sc.do("vector", lambda e: e.memset(ones1[:, :], 1.0),
                        inc="d_dv")
            t1 = sc.do("vector", lambda e: e.tensor_single_scalar(
                cc_sb[0:1, 0:4], lsi_sb[0:1, 0:4], cb_sb[0:1, 0:1], ALU.is_gt),
                waits=[tl1, tl2], inc="d_dv")
            t2 = sc.do("vector", lambda e: e.tensor_mul(
                cc_sb[0:1, 4:5], cc_sb[0:1, 0:1], cc_sb[0:1, 1:2]),
                waits=[t1], inc="d_dv")
            t3 = sc.do("vector", lambda e: e.tensor_add(
                cc_sb[0:1, 5:6], cc_sb[0:1, 1:2], cc_sb[0:1, 4:5]),
                waits=[t2], inc="d_dv")
            t4 = sc.do("vector", lambda e: e.tensor_scalar(
                cc_sb[0:1, 5:6], cc_sb[0:1, 5:6], 2.0, -1.0,
                ALU.subtract, ALU.mult), waits=[t3], inc="d_dv")
            t5 = sc.do("vector", lambda e: e.tensor_mul(
                cc_sb[0:1, 5:6], cc_sb[0:1, 5:6], cc_sb[0:1, 2:3]),
                waits=[t4], inc="d_dv")
            t6 = sc.do("vector", lambda e: e.tensor_scalar(
                cc_sb[0:1, 6:7], cc_sb[0:1, 2:3], 1.0, -1.0,
                ALU.subtract, ALU.mult), waits=[t1], inc="d_dv")
            t7 = sc.do("vector", lambda e: e.tensor_scalar(
                cc_sb[0:1, 7:8], cc_sb[0:1, 3:4], 4.0, -1.0,
                ALU.subtract, ALU.mult), waits=[t1], inc="d_dv")
            t8 = sc.do("vector", lambda e: e.tensor_mul(
                cc_sb[0:1, 6:7], cc_sb[0:1, 6:7], cc_sb[0:1, 7:8]),
                waits=[t6, t7], inc="d_dv")
            t9 = sc.do("vector", lambda e: e.tensor_add(
                cc_sb[0:1, 6:7], cc_sb[0:1, 6:7], cc_sb[0:1, 5:6]),
                waits=[t5, t8], inc="d_dv")
            tb1 = sc.do("tensor", lambda e: e.matmul(
                ps_s[0:3, 0:1], ones1[0:1, 0:3], cc_sb[0:1, 6:7],
                start=True, stop=True), waits=[t9, to2], inc="d_pe")
            t10 = sc.do("vector", lambda e: e.tensor_sub(
                m3[0:3, 0:1], ps_s[0:3, 0:1], c3_sb[0:3, 0:1]),
                waits=[tb1, tl3], inc="d_dv")
            t11 = sc.do("scalar", lambda e: e.activation(
                m3[0:3, 0:1], m3[0:3, 0:1], AF.Square, bias=0.0, scale=1.0),
                waits=[t10], inc="d_ac")
            t12 = sc.do("vector", lambda e: e.tensor_scalar(
                m3[0:3, 0:1], m3[0:3, 0:1], 1.0, -1.0,
                ALU.subtract, ALU.mult), waits=[t11], inc="d_dv")
            t13 = sc.do("vector", lambda e: e.tensor_relu(
                m3[0:3, 0:1], m3[0:3, 0:1]), waits=[t12], inc="d_dv")
            t14 = sc.do("vector", lambda e: e.tensor_scalar(
                me_sb[0:3, :], e96_sb[0:3, :], m3[0:3, 0:1], None, ALU.mult),
                waits=[t13, tl4], inc="d_dv")
            tb2 = sc.do("tensor", lambda e: e.matmul(
                ps_s[:, 0:96], ones3[0:3, :], me_sb[0:3, :],
                start=True, stop=True), waits=[t14, to1], inc="d_pe")
            t15 = sc.do("vector", lambda e: e.tensor_copy(
                m96[:, :], ps_s[:, 0:96]), waits=[tb2], inc="d_dv")
            tat_s = []
            for c in range(NCH):
                tt = sc.do("vector", lambda e, c=c: e.tensor_mul(
                    at_sb[:, c * 96:(c + 1) * 96],
                    at_sb[:, c * 96:(c + 1) * 96], m96[:, :]),
                    waits=[t15, tat] if c == 0 else [], inc="d_dv")
                tat_s.append(tt)
            r_rt = Res()
            r_pss = Res()
            r_psr = Res()
            r_psd = [Res() for _ in range(4)]
            for i in range(NT):
                tt = None
                for c in range(NCH):
                    def f(e, c=c, i=i):
                        return e.matmul(
                            ps_s[:, 0:96],
                            ln3t[:, :].rearrange("p (c t) -> p c t", c=NCH)[
                                :, c, i * 128:(i + 1) * 128],
                            at_sb[:, c * 96:(c + 1) * 96],
                            start=(c == 0), stop=(c == NCH - 1))
                    ww = [tat_s[-1], t15, tb2] if i == 0 else []
                    if i > 0:
                        ww = r_pss.for_write()
                    tt = sc.do("tensor", f, waits=ww if c == 0 else [],
                               inc="d_pe" if c == NCH - 1 else None)
                r_pss.write(tt)
                te = sc.do("vector", lambda e: e.tensor_copy(
                    rtmp[:, :], ps_s[:, 0:96]), waits=[tt], inc="d_dv")
                r_pss.read(te)
                ttp = sc.do("tensor", lambda e: e.transpose(
                    ps_r[0:96, :], rtmp[:, :], idn_sb[:, :]),
                    waits=[te, t_idn] + r_psr.for_write(), inc="d_pe")
                r_psr.write(ttp)
                tev = sc.do("vector", lambda e, i=i: e.tensor_copy(
                    rt_sb[0:96, i * 128:(i + 1) * 128], ps_r[0:96, :]),
                    waits=[ttp], inc="d_dv")
                r_psr.read(tev)
                r_rt.w.append(tev)

            sc.barrier()
            if "m96" in dbg:
                tdm = sc.do("sync", lambda e: e.dma_start(
                    dbg["m96"][:, :], m96[:, :]), inc="dbgd", dma=True)
                sc.do("sync", lambda e: None, waits=[tdm])
                if "cc" in dbg:
                    td2 = sc.do("sync", lambda e: e.dma_start(
                        dbg["cc"][:, :], cc_sb[:, :]), inc="dbgd", dma=True)
                    td3 = sc.do("sync", lambda e: e.dma_start(
                        dbg["m3"][:, :], m3[:, :]), inc="dbgd", dma=True)
                    sc.do("sync", lambda e: None, waits=[td3])
                sc.barrier()
            sc.flush("phaseD1")
            ph1.close()
            wt_sb = [pb(f"d_wt{j}", [128, NCH * 512], BF16) for j in range(2)]
            r_wt = [Res() for _ in range(2)]
            r_bt = [Res() for _ in range(2)]
            r_bia = Res()
            r_bb = Res()
            r_y = [Res() for _ in range(2)]

            def load_wt(ob, j, waits):
                tok = None
                for hh in range(2):
                    def f(e, ob=ob, j=j, hh=hh):
                        src = wt.rearrange("(c p) o -> p c o", p=128)[
                            :, hh * 16:(hh + 1) * 16,
                            ob * 512:(ob + 1) * 512]
                        dst = wt_sb[j][:, :].rearrange(
                            "p (c o) -> p c o", c=NCH)[
                            :, hh * 16:(hh + 1) * 16, :]
                        return e.dma_start(dst, src)
                    tok = sc.do("gpsimd", f, waits=waits if hh == 0 else [],
                                inc=f"d_wt{j}", dma=True)
                return tok

            def load_bt(ob, j, waits):
                return sc.do("gpsimd", lambda e, ob=ob, j=j: e.dma_start(
                    bts[j][:, :], lbt[0:96, ob * 512:(ob + 1) * 512]),
                    waits=waits, inc=f"d_bt{j}", dma=True)

            t0 = load_wt(0, 0, [])
            r_wt[0].write(t0)
            tb0 = load_bt(0, 0, [])
            r_bt[0].write(tb0)
            for ob in range(8):
                j = ob % 2
                if ob + 1 < 8:
                    jn = (ob + 1) % 2
                    tn_ = load_wt(ob + 1, jn, r_wt[jn].for_write())
                    r_wt[jn].write(tn_)
                    tnb = load_bt(ob + 1, jn, r_bt[jn].for_write())
                    r_bt[jn].write(tnb)
                tl5 = sc.do("sync", lambda e, ob=ob: e.dma_start(
                    bia_s[:, :], bia[0:1, ob * 512:(ob + 1) * 512]),
                    waits=r_bia.for_write(), inc="d_bl", dma=True)
                r_bia.write(tl5)
                tbb = sc.do("tensor", lambda e: e.matmul(
                    ps_s[:, 0:512], ones1[0:1, :], bia_s[0:1, :],
                    start=True, stop=True),
                    waits=[tl5, to2] + r_pss.for_write(), inc="d_pe")
                r_bia.read(tbb)
                r_pss.write(tbb)
                tbe = sc.do("vector", lambda e: e.tensor_copy(
                    bb_sb[:, :], ps_s[:, 0:512]),
                    waits=[tbb] + r_bb.for_write(), inc="d_dv")
                r_pss.read(tbe)
                r_bb.write(tbe)
                for i in range(NT):
                    reg = (ob * NT + i) % 4
                    w_ = (r_wt[j].for_read() + r_psd[reg].for_write()
                          + (r_rt.w[-1:] if ob == 0 and i == 0 else []))
                    tt = None
                    for c in range(NCH):
                        def f(e, c=c, i=i, j=j, reg=reg):
                            return e.matmul(
                                ps_d[:, reg * 512:reg * 512 + 512],
                                ln3t[:, :].rearrange(
                                    "p (c t) -> p c t", c=NCH)[
                                    :, c, i * 128:(i + 1) * 128],
                                wt_sb[j][:, :].rearrange(
                                    "p (c o) -> p c o", c=NCH)[:, c, :],
                                start=(c == 0), stop=False)
                        tt = sc.do("tensor", f, waits=w_ if c == 0 else [])
                    def flo(e, i=i, j=j, reg=reg):
                        return e.matmul(
                            ps_d[:, reg * 512:reg * 512 + 512],
                            rt_sb[0:96, i * 128:(i + 1) * 128],
                            bts[j][0:96, :],
                            start=False, stop=True)
                    tt = sc.do("tensor", flo,
                               waits=r_bt[j].for_read() if i == 0 else [],
                               inc="d_pe")
                    r_bt[j].read(tt)
                    r_wt[j].read(tt)
                    r_psd[reg].write(tt)

                    yj = i % 2
                    def ev(e, reg=reg, yj=yj):
                        return e.tensor_add(
                            y_sb[yj][:, :], ps_d[:, reg * 512:reg * 512 + 512],
                            bb_sb[:, :])
                    te = sc.do("vector", ev,
                               waits=[tt, tbe] + r_y[yj].for_write(),
                               inc="d_dv")
                    r_psd[reg].read(te)
                    r_bb.read(te)
                    r_y[yj].write(te)

                    def fst(e, yj=yj, i=i, ob=ob):
                        return e.dma_start(
                            y_d[i * 128:(i + 1) * 128,
                                ob * 512:(ob + 1) * 512],
                            y_sb[yj][:, :])
                    ts_ = sc.do("sync", fst, waits=[te], inc=f"d_yd{yj}",
                                dma=True)
                    r_y[yj].read(ts_)

            sc.barrier()
            if "rt" in dbg:
                tdr = sc.do("sync", lambda e: e.dma_start(
                    dbg["rt"][:, :], rt_sb[:, :]), inc="dbgd", dma=True)
                sc.do("sync", lambda e: None, waits=[tdr])
            dbg_copy("y", y_d)
            sc.barrier()
            sc.flush("phaseD")
    ln3_stack.close()

    if "E" in PHASES:
        with ExitStack() as ph:
            pb = lambda n, s_, d_: ph.enter_context(nc.sbuf_tensor(n, s_, d_))
            y_f = [pb(f"e_y{j}", [128, D], F32) for j in range(2)]
            x2_f = [pb(f"e_x{j}", [128, D], F32) for j in range(2)]
            bn6 = pb("e_bn6", [128, 48], F32)
            mv = pb("e_mv", [128, 2], F32)
            rstd = pb("e_rstd", [128, 1], F32)

            r_y = [Res() for _ in range(2)]
            r_x = [Res() for _ in range(2)]
            r_st = Res()

            for i in range(NT):
                j = i % 2
                ty = sc.do("sync", lambda e, j=j, i=i: e.dma_start(
                    y_f[j][:, :], y_d[i * 128:(i + 1) * 128, :]),
                    waits=r_y[j].for_write(), inc=f"e_y{j}", dma=True)
                r_y[j].write(ty)
                tx = sc.do("sync", lambda e, j=j, i=i: e.dma_start(
                    x2_f[j][:, :], x2_d[i * 128:(i + 1) * 128, :]),
                    waits=r_x[j].for_write(), inc=f"e_x{j}", dma=True)
                r_x[j].write(tx)
                _, trs = _ln_tile(sc, "e", y_f[j][:, :], bn6, mv, rstd,
                                  eps_sb, [t_eps], r_y[j], r_st)
                tn = sc.do("vector", lambda e, j=j: e.tensor_scalar(
                    y_f[j][:, :], y_f[j][:, :], mv[:, 0:1], rstd[:, 0:1],
                    ALU.subtract, ALU.mult), waits=[trs], inc="e_dv")
                r_y[j].write(tn)
                r_st.read(tn)
                tn2 = sc.do("vector", lambda e, j=j: e.tensor_add(
                    x2_f[j][:, :], x2_f[j][:, :], y_f[j][:, :]),
                    waits=[tn, tx], inc="e_dv")
                r_x[j].write(tn2)
                r_y[j].read(tn2)
                ts_ = sc.do("sync", lambda e, j=j, i=i: e.dma_start(
                    out[i * 128:(i + 1) * 128, :], x2_f[j][:, :]),
                    waits=[tn2], inc=f"e_out{j}", dma=True)
                r_x[j].read(ts_)
            sc.do("sync", lambda e: None,
                  waits=[("e_out0", sc.cnt["e_out0"]),
                         ("e_out1", sc.cnt["e_out1"])])
            sc.flush("phaseE")

    gx.close()
    return nc


def kernel(
    L0_attn_hidden_states,
    L0_attn_Wq,
    L0_attn_Wk,
    L0_attn_Wv,
    L0_attn_Wo,
    L0_ffn_weight,
    L0_ffn_bias,
    L0_ffn_lora_A,
    L0_ffn_lora_B,
    L0_ffn_lora_start_indices,
):
    consts = _host_consts()
    x = np.ascontiguousarray(np.asarray(L0_attn_hidden_states, dtype=np.float32))
    wq_ = np.ascontiguousarray(np.asarray(L0_attn_Wq, dtype=np.float32))
    wk_ = np.ascontiguousarray(np.asarray(L0_attn_Wk, dtype=np.float32))
    wv_ = np.ascontiguousarray(np.asarray(L0_attn_Wv, dtype=np.float32))
    wo_ = np.ascontiguousarray(np.asarray(L0_attn_Wo, dtype=np.float32))
    wt_ = np.ascontiguousarray(np.asarray(L0_ffn_weight, dtype=np.float32).T)
    bia_ = np.ascontiguousarray(
        np.asarray(L0_ffn_bias, dtype=np.float32))[None, :]
    la = np.asarray(L0_ffn_lora_A, dtype=np.float32)
    lb = np.asarray(L0_ffn_lora_B, dtype=np.float32)
    lat_ = np.zeros((D, 96), dtype=np.float32)
    lbt_ = np.zeros((96, D), dtype=np.float32)
    for n in range(3):
        lat_[:, n * 32: n * 32 + 16] = la[n].T
        lbt_[n * 32: n * 32 + 16, :] = lb[n].T
    lsi_ = np.ascontiguousarray(
        np.asarray(L0_ffn_lora_start_indices).astype(np.float32))[None, :]

    nc = build_nc()
    in_maps = []
    for b in range(8):
        in_maps.append({
            "x": x[b], "wq": wq_, "wk": wk_, "wv": wv_, "wo": wo_, "wt": wt_,
            "bia": bia_, "lat": lat_, "lbt": lbt_, "lsi": lsi_,
            "cb": np.array([[b]], dtype=np.float32),
            "csq": consts["cs_q"], "csk": consts["cs_k"],
            "tri": consts["tri"], "idn": consts["idn"],
            "e96": consts["e96"], "c3": consts["c3"],
        })
    res = run_bass_kernel_spmd(nc, in_maps, core_ids=list(range(8)))
    return np.stack([np.asarray(r["out"], dtype=np.float32) for r in res.results])


# version 16
# speedup vs baseline: 1.1067x; 1.0120x over previous
import os
from collections import defaultdict
from contextlib import ExitStack

import numpy as np
import ml_dtypes

import concourse.bass as bass
import concourse.mybir as mybir
from concourse.bass_utils import run_bass_kernel_spmd

F32 = mybir.dt.float32
BF16 = mybir.dt.bfloat16
AF = mybir.ActivationFunctionType
ALU = mybir.AluOpType
AX = mybir.AxisListType
BFNP = ml_dtypes.bfloat16

S = 2048
D = 4096
NT = S // 128
NCH = D // 128
H = 16
HD = 256
ROT = 64
EPS = 1e-5
MASKNEG = -3.0e38

ENGINES = ("sync", "scalar", "vector", "tensor", "gpsimd")

DEBUG = os.environ.get("MK_DEBUG", "")
PHASES = os.environ.get("MK_PHASES", "A,B1,B2,B3,C,C2,D,E").split(",")
NHEADS = int(os.environ.get("MK_NHEADS", str(H)))
MK_NT = int(os.environ.get("MK_NT", str(NT)))
A_LVL = int(os.environ.get("MK_ALVL", "9"))


class Res:

    def __init__(self):
        self.w = []
        self.r = []

    def write(self, tok):
        self.w = [tok]
        self.r = []

    def read(self, tok):
        self.r.append(tok)

    def for_read(self):
        return list(self.w)

    def for_write(self):
        return list(self.w) + list(self.r)


class Sched:
    def __init__(self, nc):
        self.nc = nc
        self.q = {e: [] for e in ENGINES}
        self.cnt = defaultdict(int)
        self.sems = {}
        self.pend = {e: [] for e in ENGINES}
        self.active = {e: False for e in ENGINES}

    def sem(self, name):
        if name not in self.sems:
            self.sems[name] = self.nc.alloc_semaphore(name)
        return self.sems[name]

    def do(self, eng, f, waits=(), inc=None, dma=False):
        wd = {}
        for w_ in waits:
            assert w_ is not None, "None token in waits (producer missing inc)"
        for s, v in waits:
            if v > 0:
                wd[s] = max(wd.get(s, 0), v)
        tok = None
        if inc is not None:
            k = 16 if dma else 1
            self.cnt[inc] += k
            tok = (inc, self.cnt[inc])
        rec = {"f": f, "waits": list(self.pend[eng]) + sorted(wd.items()),
               "inc": (inc, 16 if dma else 1) if inc else None}
        self.pend[eng] = []
        self.q[eng].append(rec)
        self.active[eng] = True
        return tok

    def barrier(self):
        for e in ("scalar", "vector", "tensor"):
            if self.active[e]:
                sname = "prog_" + e
                self.cnt[sname] += 1
                v = self.cnt[sname]
                self.q[e].append({"f": None, "waits": [], "inc": None,
                                  "semset": (sname, v)})
        targets = [(s, v) for s, v in self.cnt.items() if v > 0]
        for e in ENGINES:
            self.pend[e] = list(targets)
            self.active[e] = False

    def flush(self, name):
        nc = self.nc
        if not any(self.q[e] for e in ENGINES):
            return
        with nc.Block(name) as blk:
            for e in ENGINES:
                ops = self.q[e]
                if not ops:
                    continue

                def body(engobj, ops=ops):
                    for rec in ops:
                        for s, v in rec["waits"]:
                            engobj.wait_ge(self.sem(s), v)
                        if rec.get("semset"):
                            s, v = rec["semset"]
                            engobj.sem_inc(self.sem(s), 1)
                            continue
                        ins = rec["f"](engobj)
                        if rec["inc"]:
                            s, k = rec["inc"]
                            ins.then_inc(self.sem(s), k)

                getattr(blk, e)(body)
        self.q = {e: [] for e in ENGINES}


def _host_consts():
    inv_freq = (
        1.0
        / np.power(
            np.float32(10000.0),
            np.arange(0, ROT, 2, dtype=np.float32) / np.float32(ROT),
        )
    ).astype(np.float32)
    t = np.arange(S, dtype=np.float32)
    sinu = np.einsum("i,j->ij", t, inv_freq).astype(np.float32)
    sin = np.sin(sinu).astype(np.float32)
    cos = np.cos(sinu).astype(np.float32)
    cos_f = np.repeat(cos.T, 2, axis=0)
    sin_f = np.repeat(sin.T, 2, axis=0)
    cs_q = np.concatenate([cos_f / 16.0, sin_f / 16.0], axis=0).astype(np.float32)
    cs_k = np.concatenate([cos_f, sin_f], axis=0).astype(np.float32)
    tri = np.where(
        np.arange(128)[None, :] <= np.arange(128)[:, None], 0.0, MASKNEG
    ).astype(np.float32)
    idn = np.eye(128, dtype=np.float32)
    e96 = np.zeros((3, 96), dtype=np.float32)
    for n in range(3):
        e96[n, n * 32: n * 32 + 16] = 1.0
    c3 = np.array([[2.0], [3.0], [4.0]], dtype=np.float32)
    return {"cs_q": cs_q, "cs_k": cs_k, "tri": tri, "idn": idn,
            "e96": e96, "c3": c3}


def _ln_tile(sc, nm, x_ap, bn6, mv, rstd, eps_sb, deps, res_x, res_stats,
             lvl=9):
    w_x = res_x.for_read() + deps
    w_s = res_stats.for_write()
    tk = None
    for u in range(8):
        tk = sc.do("vector", lambda e, u=u: e.bn_stats(
            bn6[:, u * 6:(u + 1) * 6], x_ap[:, u * 512:(u + 1) * 512]),
            waits=(w_x + w_s) if u == 0 else [], inc=nm + "_dv")
    res_x.read(tk)
    ag = sc.do("vector", lambda e: e.bn_aggr(mv[:, :], bn6[:, :]),
               waits=[tk], inc=nm + "_dv")
    if lvl < 3:
        return ag, ag
    t3 = sc.do("scalar", lambda e: e.activation(
        rstd[:, :], mv[:, 1:2], AF.Ln, bias=eps_sb[:, 0:1], scale=1.0),
        waits=[ag], inc=nm + "_ac")
    t4 = sc.do("scalar", lambda e: e.activation(
        rstd[:, :], rstd[:, :], AF.Exp, bias=0.0, scale=-0.5),
        waits=[t3], inc=nm + "_ac")
    res_stats.write(t4)
    return ag, t4


def build_nc():
    nc = bass.Bass("TRN2", target_bir_lowering=False)

    x_in = nc.dram_tensor("x", [S, D], F32, kind="ExternalInput").ap()
    wq = nc.dram_tensor("wq", [D, D], F32, kind="ExternalInput").ap()
    wk = nc.dram_tensor("wk", [D, D], F32, kind="ExternalInput").ap()
    wv = nc.dram_tensor("wv", [D, D], F32, kind="ExternalInput").ap()
    wo = nc.dram_tensor("wo", [D, D], F32, kind="ExternalInput").ap()
    wt = nc.dram_tensor("wt", [D, D], F32, kind="ExternalInput").ap()
    bia = nc.dram_tensor("bia", [1, D], F32, kind="ExternalInput").ap()
    lat = nc.dram_tensor("lat", [D, 96], F32, kind="ExternalInput").ap()
    lbt = nc.dram_tensor("lbt", [96, D], F32, kind="ExternalInput").ap()
    lsi = nc.dram_tensor("lsi", [1, 4], F32, kind="ExternalInput").ap()
    cb = nc.dram_tensor("cb", [1, 1], F32, kind="ExternalInput").ap()
    csq = nc.dram_tensor("csq", [128, S], F32, kind="ExternalInput").ap()
    csk = nc.dram_tensor("csk", [128, S], F32, kind="ExternalInput").ap()
    tri = nc.dram_tensor("tri", [128, 128], F32, kind="ExternalInput").ap()
    idn = nc.dram_tensor("idn", [128, 128], F32, kind="ExternalInput").ap()
    e96 = nc.dram_tensor("e96", [3, 96], F32, kind="ExternalInput").ap()
    c3 = nc.dram_tensor("c3", [3, 1], F32, kind="ExternalInput").ap()

    out = nc.dram_tensor("out", [S, D], F32, kind="ExternalOutput").ap()
    qt_d = nc.dram_tensor("qt_d", [D, S], BF16, kind="Internal").ap()
    kt_d = nc.dram_tensor("kt_d", [D, S], BF16, kind="Internal").ap()
    v_d = nc.dram_tensor("v_d", [S, D], BF16, kind="Internal").ap()
    ot_d = nc.dram_tensor("ot_d", [D, S], BF16, kind="Internal").ap()
    attn_d = nc.dram_tensor("attn_d", [S, D], F32, kind="Internal").ap()
    x2_d = nc.dram_tensor("x2_d", [S, D], F32, kind="Internal").ap()
    y_d = nc.dram_tensor("y_d", [S, D], F32, kind="Internal").ap()

    dbg = {}
    for k, shape, dt in (("lnt", [D, S], BF16), ("ot", [D, S], BF16),
                         ("attn", [S, D], F32), ("x2", [S, D], F32),
                         ("y", [S, D], F32), ("m96", [128, 96], BF16),
                         ("rt", [96, S], BF16), ("cc", [1, 8], F32),
                         ("m3", [3, 1], F32)):
        if k in DEBUG:
            dbg[k] = nc.dram_tensor("dbg_" + k, shape, dt,
                                    kind="ExternalOutput").ap()

    sc = Sched(nc)
    gx = ExitStack()
    gbuf = lambda n, s_, d_: gx.enter_context(nc.sbuf_tensor(n, s_, d_))

    idn_sb = gbuf("idn_sb", [128, 128], BF16)
    eps_sb = gbuf("eps_sb", [128, 1], F32)

    t_idn = sc.do("gpsimd", lambda g: g.dma_start(idn_sb[:, :], idn[:, :]),
                  inc="c_idn", dma=True)
    t_eps = sc.do("vector", lambda v: v.memset(eps_sb[:, :], EPS), inc="c_eps")

    def dbg_copy(key, src_ap, rows=None):
        if key in dbg:
            t = sc.do("sync", lambda e, k=key, s_=src_ap, rows=rows:
                      e.dma_start(dbg[k][0:rows, :] if rows else dbg[k][:, :],
                                  s_[0:rows, :] if rows else s_[:, :]),
                      inc="dbgd", dma=True)
            sc.do("sync", lambda e: None, waits=[t])

    ab_stack = ExitStack()
    ln1t = ab_stack.enter_context(nc.sbuf_tensor("ln1t", [128, NCH * 2048], BF16))
    bq_ps = ab_stack.enter_context(nc.psum_tensor("bq_ps", [128, 2048], F32))
    r_bps = [Res() for _ in range(4)]
    a_done = [None] * NT
    if "A" in PHASES:
        with ExitStack() as ph:
            pb = lambda n, s_, d_: ph.enter_context(nc.sbuf_tensor(n, s_, d_))
            x_f = [pb(f"a_x{j}", [128, D], F32) for j in range(2)]
            ln_bf = [pb(f"a_ln{j}", [128, D], BF16) for j in range(2)]
            bn6 = pb("a_bn6", [128, 48], F32)
            mv = pb("a_mv", [128, 2], F32)
            rstd = pb("a_rstd", [128, 1], F32)
            ps_t = [ph.enter_context(nc.psum_tensor(f"a_pst{jj_}", [128, 512], BF16))
                    for jj_ in range(2)]

            r_x = [Res() for _ in range(2)]
            r_ln = [Res() for _ in range(2)]
            r_st = Res()
            r_ps = [Res() for _ in range(2)]

            for i in range(NT):
                j = i % 2
                tk = sc.do("sync", lambda e, j=j, i=i: e.dma_start(
                    x_f[j][:, :], x_in[i * 128:(i + 1) * 128, :]),
                    waits=r_x[j].for_write(), inc=f"a_x{j}", dma=True)
                r_x[j].write(tk)
                _, trs = _ln_tile(sc, "a", x_f[j][:, :], bn6, mv, rstd,
                                  eps_sb, [t_eps], r_x[j], r_st)
                tn = sc.do("vector", lambda e, j=j: e.tensor_scalar(
                    ln_bf[j][:, :], x_f[j][:, :], mv[:, 0:1], rstd[:, 0:1],
                    ALU.subtract, ALU.mult),
                    waits=[trs] + r_ln[j].for_write(), inc="a_dv")
                r_x[j].read(tn)
                r_st.read(tn)
                r_ln[j].write(tn)
                ev = None
                for cg in range(8):
                    half = cg % 2
                    tw = [tn, t_idn] + r_ps[half].for_write()
                    tt = None
                    for cc in range(4):
                        c = cg * 4 + cc
                        tt = sc.do("tensor", lambda e, j=j, c=c, half=half, cc=cc:
                                   e.transpose(
                                       ps_t[half][:, cc * 128:(cc + 1) * 128],
                                       ln_bf[j][:, c * 128:(c + 1) * 128],
                                       idn_sb[:, :]),
                                   waits=tw if cc == 0 else [],
                                   inc="a_pe" if cc == 3 else None)
                    r_ln[j].read(tt)
                    r_ps[half].write(tt)

                    def evac(e, cg=cg, i=i, half=half):
                        dst = ln1t[:, :].rearrange(
                            "p (c t) -> p c t", c=NCH)[
                            :, cg * 4:(cg + 1) * 4, i * 128:(i + 1) * 128]
                        src = ps_t[half][:, :].rearrange(
                            "p (c t) -> p c t", c=4)[:, :, :]
                        return e.activation(dst, src, AF.Copy,
                                            bias=0.0, scale=1.0)
                    ev = sc.do("scalar", evac, waits=[tt], inc="a_ac")
                    r_ps[half].read(ev)
                a_done[i] = ev
            sc.flush("phaseA")

    b1_fin = []
    if "B1" in PHASES:
        with ExitStack() as ph:
            pb = lambda n, s_, d_: ph.enter_context(nc.sbuf_tensor(n, s_, d_))
            w_sb = [pb(f"b1_w{j}", [128, NCH * 128], BF16) for j in range(2)]
            stg = [pb(f"b1_s{j}", [128, 512], BF16) for j in range(2)]
            rot_f = pb("b1_rot", [64, 512], F32)
            sw_f = pb("b1_sw", [64, 512], F32)
            tmp_f = pb("b1_tmp", [64, 512], F32)
            cskc = pb("b1_cc", [64, S], F32)
            csks = pb("b1_cs", [64, S], F32)

            for t_, r0 in ((cskc, 0), (csks, 64)):
                sc.do("sync", lambda e, t_=t_, r0=r0: e.dma_start(
                    t_[:, :], csk[r0:r0 + 64, :]), inc="b1_tab", dma=True)
            t_tab1 = ("b1_tab", sc.cnt["b1_tab"])

            r_w = [Res() for _ in range(2)]
            r_stg = [Res() for _ in range(2)]
            r_rot, r_sw, r_tmp = Res(), Res(), Res()
            tmz = None
            for t_ in (rot_f, sw_f, tmp_f):
                tmz = sc.do("vector", lambda e, t_=t_: e.memset(t_[:, :], 0.0),
                            inc="b1_dv")
            r_rot.write(tmz)
            r_sw.write(tmz)
            r_tmp.write(tmz)

            blocks = ([("q", ob) for ob in range(32)]
                      + [("k", ob) for ob in range(32)])

            def load_w1(idx, waits):
                m, ob = blocks[idx]
                src = wq if m == "q" else wk
                j = idx % 2
                tok = None
                for hh in range(2):
                    def f(e, src=src, ob=ob, j=j, hh=hh):
                        s_ = src.rearrange("(c p) o -> p c o", p=128)[
                            :, hh * 16:(hh + 1) * 16, ob * 128:(ob + 1) * 128]
                        d_ = w_sb[j][:, :].rearrange(
                            "p (c o) -> p c o", c=NCH)[
                            :, hh * 16:(hh + 1) * 16, :]
                        return e.dma_start(d_, s_)
                    tok = sc.do("gpsimd", f, waits=waits if hh == 0 else [],
                                inc=f"b1_w{j}", dma=True)
                return tok

            t0 = load_w1(0, [])
            r_w[0].write(t0)
            gctr = [0]
            for idx in range(64):
                m, ob = blocks[idx]
                j = idx % 2
                sca = (1.0 / 16.0) if m == "q" else 1.0
                dst_d = qt_d if m == "q" else kt_d
                if idx + 1 < 64:
                    jn = (idx + 1) % 2
                    tn_ = load_w1(idx + 1, r_w[jn].for_write())
                    r_w[jn].write(tn_)
                for tt4 in range(4):
                    reg = gctr[0] % 4
                    sj = gctr[0] % 2
                    gctr[0] += 1
                    w_ = (r_w[j].for_read() + r_bps[reg].for_write()
                          + ([a_done[tt4 * 4 + 3]]
                             if a_done[tt4 * 4 + 3] is not None else []))
                    tmm = None
                    for c in range(NCH):
                        def f(e, j=j, c=c, tt4=tt4, reg=reg):
                            return e.matmul(
                                bq_ps[:, reg * 512:(reg + 1) * 512],
                                w_sb[j][:, :].rearrange(
                                    "p (c o) -> p c o", c=NCH)[:, c, :],
                                ln1t[:, :].rearrange(
                                    "p (c t) -> p c t", c=NCH)[
                                    :, c, tt4 * 512:(tt4 + 1) * 512],
                                start=(c == 0), stop=(c == NCH - 1))
                        tmm = sc.do("tensor", f, waits=w_ if c == 0 else [],
                                    inc="b1_pe" if c == NCH - 1 else None)
                    r_w[j].read(tmm)
                    r_bps[reg].write(tmm)
                    te = sc.do("scalar", lambda e, reg=reg, sj=sj, sca=sca:
                               e.activation(
                                   stg[sj][:, :],
                                   bq_ps[:, reg * 512:(reg + 1) * 512],
                                   AF.Copy, bias=0.0, scale=sca),
                               waits=[tmm] + r_stg[sj].for_write(),
                               inc="b1_ac")
                    r_bps[reg].read(te)
                    r_stg[sj].write(te)
                    dma_wait = [te]
                    if ob % 2 == 0:
                        tr_ = sc.do("scalar", lambda e, reg=reg, sca=sca:
                                    e.activation(
                                        rot_f[0:64, :],
                                        bq_ps[0:64, reg * 512:reg * 512 + 512],
                                        AF.Copy, bias=0.0, scale=sca),
                                    waits=[tmm] + r_rot.for_write(),
                                    inc="b1_ac")
                        r_rot.write(tr_)
                        r_bps[reg].read(tr_)
                        swmask = [i ^ 1 for i in range(32)]
                        swt = sc.do("vector", lambda e, swmask=swmask:
                                    e.stream_shuffle(
                                        sw_f[0:64, :], rot_f[0:64, :], swmask),
                                    waits=[tr_] + r_sw.for_write(),
                                    inc="b1_dv")
                        r_sw.write(swt)
                        r_rot.read(swt)
                        t1 = sc.do("vector", lambda e, tt4=tt4: e.tensor_mul(
                            tmp_f[0:64, :], rot_f[0:64, :],
                            cskc[0:64, tt4 * 512:(tt4 + 1) * 512]),
                            waits=[tr_, t_tab1] + r_tmp.for_write(),
                            inc="b1_dv")
                        r_tmp.write(t1)
                        r_rot.read(t1)
                        t2 = sc.do("vector", lambda e, tt4=tt4: e.tensor_mul(
                            sw_f[0:64, :], sw_f[0:64, :],
                            csks[0:64, tt4 * 512:(tt4 + 1) * 512]),
                            waits=[swt, t_tab1], inc="b1_dv")
                        r_sw.write(t2)
                        t3 = sc.do("vector", lambda e, sj=sj: e.tensor_add(
                            stg[sj][0:64, :], tmp_f[0:64, :], sw_f[0:64, :]),
                            waits=[t1, t2, te], inc="b1_dv")
                        r_tmp.read(t3)
                        r_sw.read(t3)
                        r_stg[sj].w.append(t3)
                        dma_wait = [te, t3]
                    ts_ = sc.do("sync", lambda e, sj=sj, dst_d=dst_d,
                                ob=ob, tt4=tt4: e.dma_start(
                                    dst_d[ob * 128:(ob + 1) * 128,
                                          tt4 * 512:(tt4 + 1) * 512],
                                    stg[sj][:, :]),
                                waits=dma_wait, inc=f"b1_st{reg}", dma=True)
                    r_stg[sj].read(ts_)
            sc.flush("phaseB1")
        b1_fin = [("b1_st0", sc.cnt["b1_st0"]), ("b1_st1", sc.cnt["b1_st1"])]

    if "B2" in PHASES:
        with ExitStack() as ph:
            pb = lambda n, s_, d_: ph.enter_context(nc.sbuf_tensor(n, s_, d_))
            wv_sb = [pb(f"b2_w{j}", [128, NCH * 512], BF16) for j in range(2)]
            vst = [pb(f"b2_s{j}", [128, 512], BF16) for j in range(2)]
            r_wv = [Res() for _ in range(2)]
            r_vst = [Res() for _ in range(2)]
            alias_w = list(b1_fin) + ([a_done[NT - 1]]
                                      if a_done[NT - 1] is not None else [])

            def load_wv(ob, j, waits):
                tok = None
                for hh in range(2):
                    def f(e, ob=ob, j=j, hh=hh):
                        src = wv.rearrange("(c p) o -> p c o", p=128)[
                            :, hh * 16:(hh + 1) * 16,
                            ob * 512:(ob + 1) * 512]
                        dst = wv_sb[j][:, :].rearrange(
                            "p (c o) -> p c o", c=NCH)[
                            :, hh * 16:(hh + 1) * 16, :]
                        return e.dma_start(dst, src)
                    tok = sc.do("gpsimd", f, waits=waits if hh == 0 else [],
                                inc=f"b2_w{j}", dma=True)
                return tok

            t0 = load_wv(0, 0, alias_w)
            r_wv[0].write(t0)
            gctr2 = [0]
            for ob in range(8):
                j = ob % 2
                if ob + 1 < 8:
                    jn = (ob + 1) % 2
                    tn_ = load_wv(ob + 1, jn, r_wv[jn].for_write()
                                  + (alias_w if ob == 0 else []))
                    r_wv[jn].write(tn_)
                for i in range(NT):
                    reg = gctr2[0] % 4
                    vj = i % 2
                    gctr2[0] += 1
                    w_ = (r_wv[j].for_read() + r_bps[reg].for_write()
                          + ([a_done[i]] if a_done[i] is not None else []))
                    tmm = None
                    for c in range(NCH):
                        def f(e, c=c, i=i, j=j, reg=reg):
                            return e.matmul(
                                bq_ps[:, reg * 512:(reg + 1) * 512],
                                ln1t[:, :].rearrange(
                                    "p (c t) -> p c t", c=NCH)[
                                    :, c, i * 128:(i + 1) * 128],
                                wv_sb[j][:, :].rearrange(
                                    "p (c o) -> p c o", c=NCH)[:, c, :],
                                start=(c == 0), stop=(c == NCH - 1))
                        tmm = sc.do("tensor", f, waits=w_ if c == 0 else [],
                                    inc="b2_pe" if c == NCH - 1 else None)
                    r_wv[j].read(tmm)
                    r_bps[reg].write(tmm)
                    te = sc.do("scalar", lambda e, reg=reg, vj=vj: e.activation(
                        vst[vj][:, :], bq_ps[:, reg * 512:(reg + 1) * 512],
                        AF.Copy, bias=0.0, scale=1.0),
                        waits=[tmm] + r_vst[vj].for_write()
                        + (alias_w if (ob == 0 and i == 0) else []),
                        inc="b2_ac")
                    r_bps[reg].read(te)
                    r_vst[vj].write(te)
                    ts_ = sc.do("sync", lambda e, vj=vj, i=i, ob=ob:
                                e.dma_start(
                                    v_d[i * 128:(i + 1) * 128,
                                        ob * 512:(ob + 1) * 512],
                                    vst[vj][:, :]),
                                waits=[te], inc=f"b2_st{reg}", dma=True)
                    r_vst[vj].read(ts_)
            sc.flush("phaseB2")

    sc.barrier()
    ab_stack.close()

    if "B3" in PHASES:
        with ExitStack() as ph:
            pb = lambda n, s_, d_: ph.enter_context(nc.sbuf_tensor(n, s_, d_))
            qT = [[pb(f"b3_q{u}{b}", [128, 2 * S], BF16) for b in range(2)]
                  for u in range(2)]
            kT = [[pb(f"b3_k{u}{b}", [128, 2 * S], BF16) for b in range(2)]
                  for u in range(2)]
            v_sb = [[pb(f"b3_v{u}{b}", [128, NT * 256], BF16)
                     for b in range(2)] for u in range(2)]
            P_sb = [pb(f"b3_P{u}", [128, 2 * S], BF16) for u in range(2)]
            PT_sb = [pb(f"b3_PT{u}", [128, 1024], BF16) for u in range(2)]
            O_acc = [pb(f"b3_O{u}", [128, NT * HD], BF16) for u in range(2)]
            OT_sb = [pb(f"b3_OT{u}", [128, 2 * S], BF16) for u in range(2)]
            mx = pb("b3_mx", [128, 8], F32)
            mneg2 = pb("b3_mneg", [128, 4], F32)
            lc = pb("b3_lc", [128, 16], F32)
            rl2 = pb("b3_rl", [128, 4], F32)
            tri_sb = pb("b3_tri", [128, 128], F32)
            ps_s = ph.enter_context(nc.psum_tensor("b3_pss", [128, 2048], F32))
            ps_o = [ph.enter_context(nc.psum_tensor(f"b3_pso{u}", [128, 512],
                                                    F32)) for u in range(2)]
            ps_p = [ph.enter_context(nc.psum_tensor(f"b3_psp{u}", [128, 512],
                                                    BF16)) for u in range(2)]

            sc.do("sync", lambda e: e.dma_start(tri_sb[:, :], tri[:, :]),
                  inc="b3_tab", dma=True)
            t_tab = ("b3_tab", sc.cnt["b3_tab"])

            r_q = [[Res() for _ in range(2)] for _ in range(2)]
            r_k = [[Res() for _ in range(2)] for _ in range(2)]
            r_v = [[Res() for _ in range(2)] for _ in range(2)]
            r_P = [[Res() for _ in range(2)] for _ in range(2)]
            r_PT = [[Res() for _ in range(2)] for _ in range(2)]
            r_Oacc = [Res() for _ in range(2)]
            r_OT = [Res() for _ in range(2)]
            r_mx = [Res() for _ in range(2)]
            r_lc = [[Res() for _ in range(2)] for _ in range(2)]
            r_mneg = [[Res() for _ in range(2)] for _ in range(2)]
            r_rl = [[Res() for _ in range(2)] for _ in range(2)]
            r_sreg = [Res() for _ in range(4)]
            r_pso = [Res() for _ in range(2)]
            r_psp = [Res() for _ in range(2)]
            s_ctr = [0]

            def load_pair(g):
                b = g % 2
                sem = f"b3_l{b}"
                for u in range(2):
                    h = 2 * g + u
                    wfull = (r_q[u][b].for_write() + r_k[u][b].for_write()
                             + r_v[u][b].for_write())
                    sc.do("sync", lambda e, h=h, u=u, b=b: e.dma_start(
                        qT[u][b][:, 0:S], qt_d[h * HD:h * HD + 128, :]),
                        waits=wfull, inc=sem, dma=True)
                    sc.do("sync", lambda e, h=h, u=u, b=b: e.dma_start(
                        qT[u][b][:, S:2 * S],
                        qt_d[h * HD + 128:h * HD + 256, :]),
                        inc=sem, dma=True)
                    sc.do("gpsimd", lambda e, h=h, u=u, b=b: e.dma_start(
                        kT[u][b][:, 0:S], kt_d[h * HD:h * HD + 128, :]),
                        waits=wfull, inc=sem, dma=True)
                    sc.do("gpsimd", lambda e, h=h, u=u, b=b: e.dma_start(
                        kT[u][b][:, S:2 * S],
                        kt_d[h * HD + 128:h * HD + 256, :]),
                        inc=sem, dma=True)

                    def fv(e, h=h, u=u, b=b):
                        src = v_d.rearrange("(t p) c -> p t c", p=128)[
                            :, :, h * HD:(h + 1) * HD]
                        dst = v_sb[u][b][:, :].rearrange(
                            "p (t c) -> p t c", t=NT)
                        return e.dma_start(dst, src)
                    sc.do("sync", fv, inc=sem, dma=True)
                tok = (sem, sc.cnt[sem])
                for u in range(2):
                    r_q[u][b].write(tok)
                    r_k[u][b].write(tok)
                    r_v[u][b].write(tok)

            def emit_scores(u, b, qi):
                rowlen = (qi + 1) * 128
                nchk = (rowlen + 511) // 512
                chks = []
                ldtok = r_q[u][b].for_read() + r_k[u][b].for_read()
                for ci in range(nchk):
                    k0 = ci * 512
                    klen = min(rowlen - k0, 512)
                    reg = s_ctr[0] % 4
                    s_ctr[0] += 1
                    off = reg * 512
                    chks.append((ci, k0, klen, off, reg))
                    w_ = r_sreg[reg].for_write() + ldtok
                    tt = None
                    for st in range(2):
                        def f(e, u=u, b=b, qi=qi, k0=k0, klen=klen, st=st,
                              off=off):
                            return e.matmul(
                                ps_s[:, off:off + klen],
                                qT[u][b][:, st * S + qi * 128:
                                         st * S + (qi + 1) * 128],
                                kT[u][b][:, st * S + k0:st * S + k0 + klen],
                                start=(st == 0), stop=(st == 1))
                        tt = sc.do("tensor", f, waits=w_ if st == 0 else [],
                                   inc="b3_pe" if st == 1 else None)
                    r_q[u][b].read(tt)
                    r_k[u][b].read(tt)
                    r_sreg[reg].write(tt)
                    if k0 + klen == rowlen:
                        dlo = off + klen - 128
                        tt = sc.do("vector", lambda e, dlo=dlo: e.tensor_add(
                            ps_s[:, dlo:dlo + 128], ps_s[:, dlo:dlo + 128],
                            tri_sb[:, :]),
                            waits=[tt, t_tab], inc="b3_dv")
                        r_sreg[reg].w.append(tt)
                    tm = sc.do("vector", lambda e, u=u, off=off, klen=klen,
                               ci=ci: e.reduce_max(
                                   mx[:, u * 4 + ci:u * 4 + ci + 1],
                                   ps_s[:, off:off + klen], AX.X),
                               waits=[tt] + (r_mx[u].for_write()
                                             if ci == 0 else []),
                               inc="b3_dv")
                    r_sreg[reg].read(tm)
                    if ci == 0:
                        r_mx[u].write(tm)
                    else:
                        r_mx[u].w.append(tm)
                pj = qi % 2
                tmn = sc.do("vector", lambda e, u=u, nchk=nchk, pj=pj:
                            e.reduce_max(
                                mneg2[:, u * 2 + pj:u * 2 + pj + 1],
                                mx[:, u * 4:u * 4 + nchk], AX.X, negate=True),
                            waits=r_mx[u].w + r_mneg[u][pj].for_write(),
                            inc="b3_dv")
                r_mx[u].read(tmn)
                r_mneg[u][pj].write(tmn)
                return {"u": u, "b": b, "qi": qi, "pj": pj, "tmn": tmn,
                        "chks": chks}

            def emit_softmax(st_):
                u, qi, pj, tmn, chks = (st_["u"], st_["qi"], st_["pj"],
                                        st_["tmn"], st_["chks"])
                nchk = len(chks)
                etoks = []
                for (ci, k0, klen, off, reg) in chks:
                    def fe(e, u=u, off=off, klen=klen, k0=k0, ci=ci, pj=pj):
                        return e.activation(
                            P_sb[u][:, pj * S + k0:pj * S + k0 + klen],
                            ps_s[:, off:off + klen], AF.Exp,
                            bias=mneg2[:, u * 2 + pj:u * 2 + pj + 1],
                            scale=1.0,
                            accum_out=lc[:, u * 8 + pj * 4 + ci:
                                         u * 8 + pj * 4 + ci + 1])
                    te = sc.do("scalar", fe,
                               waits=[tmn] + (r_P[u][pj].for_write()
                                              + r_lc[u][pj].for_write()
                                              if ci == 0 else []),
                               inc="b3_ac")
                    etoks.append(te)
                    r_sreg[reg].read(te)
                    r_mneg[u][pj].read(te)
                    if ci == 0:
                        r_P[u][pj].write(te)
                        r_lc[u][pj].write(te)
                    else:
                        r_P[u][pj].w.append(te)
                        r_lc[u][pj].w.append(te)

                tl = sc.do("vector", lambda e, u=u, nchk=nchk, pj=pj:
                           e.reduce_sum(
                               rl2[:, u * 2 + pj:u * 2 + pj + 1],
                               lc[:, u * 8 + pj * 4:u * 8 + pj * 4 + nchk],
                               AX.X),
                           waits=etoks + r_rl[u][pj].for_write(),
                           inc="b3_dv")
                r_lc[u][pj].read(tl)
                trl = sc.do("vector", lambda e, u=u, pj=pj: e.reciprocal(
                    rl2[:, u * 2 + pj:u * 2 + pj + 1],
                    rl2[:, u * 2 + pj:u * 2 + pj + 1]),
                    waits=[tl], inc="b3_dv")
                r_rl[u][pj].write(trl)
                st_["etoks"] = etoks
                st_["trl"] = trl
                return st_

            def emit_pv(st_):
                u, b, qi, pj = st_["u"], st_["b"], st_["qi"], st_["pj"]
                etoks = st_["etoks"]
                trl = st_["trl"]
                ngrp = (qi + 4) // 4
                pv_tok = None
                for gp in range(ngrp):
                    jlo = gp * 4
                    jhi = min(qi + 1, jlo + 4)
                    pbh = gp % 2
                    tt = None
                    for jj_ in range(jlo, jhi):
                        def ft(e, u=u, pj=pj, jj_=jj_):
                            return e.transpose(
                                ps_p[u][:, (jj_ % 4) * 128:
                                       (jj_ % 4 + 1) * 128],
                                P_sb[u][:, pj * S + jj_ * 128:
                                        pj * S + (jj_ + 1) * 128],
                                idn_sb[:, :])
                        tt = sc.do("tensor", ft,
                                   waits=([etoks[gp], t_idn]
                                          + r_psp[u].for_write())
                                   if jj_ == jlo else [],
                                   inc="b3_pe" if jj_ == jhi - 1 else None)
                    r_P[u][pj].read(tt)
                    r_psp[u].write(tt)

                    def fev(e, u=u, pbh=pbh, n=(jhi - jlo)):
                        return e.tensor_copy(
                            PT_sb[u][:, pbh * 512:pbh * 512 + n * 128],
                            ps_p[u][:, 0:n * 128])
                    tev = sc.do("vector", fev,
                                waits=[tt] + r_PT[u][pbh].for_write(),
                                inc="b3_dv")
                    r_psp[u].read(tev)
                    r_PT[u][pbh].write(tev)
                    for jj_ in range(jlo, jhi):
                        def fpv(e, u=u, b=b, pbh=pbh, jj_=jj_, qi=qi):
                            return e.matmul(
                                ps_o[u][:, 0:HD],
                                PT_sb[u][:, pbh * 512 + (jj_ % 4) * 128:
                                        pbh * 512 + (jj_ % 4 + 1) * 128],
                                v_sb[u][b][:, jj_ * 256:(jj_ + 1) * 256],
                                start=(jj_ == 0), stop=(jj_ == qi),
                                skip_group_check=True)
                        ww = [tev] + r_v[u][b].for_read()
                        if jj_ == 0:
                            ww = ww + r_pso[u].for_write()
                        pv_tok = sc.do("tensor", fpv,
                                       waits=ww if jj_ in (jlo, 0) else [],
                                       inc="b3_pe" if jj_ == jhi - 1
                                       else None)
                    r_PT[u][pbh].read(pv_tok)
                r_v[u][b].read(pv_tok)
                r_pso[u].write(pv_tok)

                tof = sc.do("vector", lambda e, u=u, pj=pj, qi=qi:
                            e.tensor_scalar(
                                O_acc[u][:, qi * HD:(qi + 1) * HD],
                                ps_o[u][:, 0:HD],
                                rl2[:, u * 2 + pj:u * 2 + pj + 1], None,
                                ALU.mult),
                            waits=[pv_tok, trl]
                            + (r_Oacc[u].for_write() if qi == 0 else []),
                            inc="b3_dv")
                r_rl[u][pj].read(tof)
                r_pso[u].read(tof)
                if qi == 0:
                    r_Oacc[u].write(tof)
                else:
                    r_Oacc[u].w.append(tof)

            load_pair(0)
            for g in range(8):
                b = g % 2
                prevA = prevB = None
                for qi in range(NT):
                    curA = emit_scores(0, b, qi)
                    if qi == 1 and g + 1 < 8:
                        load_pair(g + 1)
                    if prevB is not None:
                        emit_pv(prevB)
                    curA = emit_softmax(curA)
                    curB = emit_scores(1, b, qi)
                    if prevA is not None:
                        emit_pv(prevA)
                    curB = emit_softmax(curB)
                    prevA, prevB = curA, curB
                emit_pv(prevA)
                emit_pv(prevB)
                for u in range(2):
                    for st in range(2):
                        for grp in range(4):
                            tt = None
                            for k in range(4):
                                qi_ = grp * 4 + k

                                def ftr(e, u=u, st=st, qi_=qi_, k=k):
                                    return e.transpose(
                                        ps_p[u][:, k * 128:(k + 1) * 128],
                                        O_acc[u][:, qi_ * HD + st * 128:
                                               qi_ * HD + (st + 1) * 128],
                                        idn_sb[:, :])
                                tt = sc.do("tensor", ftr,
                                           waits=([r_Oacc[u].w[-1], t_idn]
                                                  + r_psp[u].for_write())
                                           if k == 0 else [],
                                           inc="b3_pe" if k == 3 else None)
                            r_Oacc[u].read(tt)
                            r_psp[u].write(tt)

                            def fevo(e, u=u, st=st, grp=grp):
                                return e.tensor_copy(
                                    OT_sb[u][:, st * S + grp * 512:
                                           st * S + (grp + 1) * 512],
                                    ps_p[u][:, 0:512])
                            tev = sc.do("vector", fevo,
                                        waits=[tt] + (r_OT[u].for_write()
                                                      if (st == 0
                                                          and grp == 0)
                                                      else []),
                                        inc="b3_dv")
                            r_psp[u].read(tev)
                            if st == 0 and grp == 0:
                                r_OT[u].write(tev)
                            else:
                                r_OT[u].w.append(tev)
                for u in range(2):
                    h = 2 * g + u
                    st_tok = None
                    for s_ in range(2):
                        def fst(e, u=u, s_=s_, h=h):
                            return e.dma_start(
                                ot_d[h * HD + s_ * 128:
                                     h * HD + (s_ + 1) * 128, :],
                                OT_sb[u][:, s_ * S:(s_ + 1) * S])
                        st_tok = sc.do("sync", fst,
                                       waits=r_OT[u].for_write()
                                       if s_ == 0 else [],
                                       inc="b3_otd", dma=True)
                    r_OT[u].read(st_tok)

            sc.barrier()
            dbg_copy("ot", ot_d)
            sc.barrier()
            sc.flush("phaseB3")


    if "C" in PHASES:
        with ExitStack() as ph:
            pb = lambda n, s_, d_: ph.enter_context(nc.sbuf_tensor(n, s_, d_))
            ot_sb = pb("c_ot", [128, NCH * 2048], BF16)
            wo_sb = [pb(f"c_wo{j}", [128, NCH * 512], BF16) for j in range(2)]
            at_sb = [pb(f"c_at{j}", [128, 512], F32) for j in range(2)]
            ps_c = ph.enter_context(nc.psum_tensor("c_ps", [128, 1024], F32))

            r_ot, r_wo = Res(), [Res() for _ in range(2)]
            r_at = [Res() for _ in range(2)]
            r_psc = [Res() for _ in range(2)]

            tokh = None
            tok = None
            for th in range(2):
                for qd in range(4):
                    def f(e, qd=qd, th=th):
                        src = ot_d.rearrange("(c p) t -> p c t", p=128)[
                            :, qd * 8:(qd + 1) * 8,
                            th * 1024:(th + 1) * 1024]
                        dst = ot_sb[:, :].rearrange(
                            "p (c t) -> p c t", c=NCH)[
                            :, qd * 8:(qd + 1) * 8,
                            th * 1024:(th + 1) * 1024]
                        return e.dma_start(dst, src)
                    tok = sc.do("sync" if qd % 2 == 0 else "gpsimd", f,
                                inc="c_otl", dma=True)
                if th == 0:
                    tokh = ("c_otl", sc.cnt["c_otl"])
            r_ot.write(tok)
            r_oth = Res()
            r_oth.write(tokh)

            def load_wo(ob, j, waits):
                tok = None
                for hh in range(2):
                    def f(e, ob=ob, j=j, hh=hh):
                        src = wo.rearrange("(c p) o -> p c o", p=128)[
                            :, hh * 16:(hh + 1) * 16,
                            ob * 512:(ob + 1) * 512]
                        dst = wo_sb[j][:, :].rearrange(
                            "p (c o) -> p c o", c=NCH)[
                            :, hh * 16:(hh + 1) * 16, :]
                        return e.dma_start(dst, src)
                    tok = sc.do("gpsimd", f, waits=waits if hh == 0 else [],
                                inc=f"c_wo{j}d", dma=True)
                return tok

            t0 = load_wo(0, 0, [])
            r_wo[0].write(t0)
            for ob in range(8):
                j = ob % 2
                if ob + 1 < 8:
                    jn = (ob + 1) % 2
                    tn_ = load_wo(ob + 1, jn, r_wo[jn].for_write())
                    r_wo[jn].write(tn_)
                for i in range(NT):
                    reg = i % 2
                    w_ = (r_wo[j].for_read()
                          + (r_oth.for_read() if i < 8 else r_ot.for_read())
                          + r_psc[reg].for_write())
                    tt = None
                    for c in range(NCH):
                        def f(e, c=c, i=i, j=j, reg=reg):
                            return e.matmul(
                                ps_c[:, reg * 512:reg * 512 + 512],
                                ot_sb[:, :].rearrange(
                                    "p (c t) -> p c t", c=NCH)[
                                    :, c, i * 128:(i + 1) * 128],
                                wo_sb[j][:, :].rearrange(
                                    "p (c o) -> p c o", c=NCH)[:, c, :],
                                start=(c == 0), stop=(c == NCH - 1))
                        tt = sc.do("tensor", f, waits=w_ if c == 0 else [],
                                   inc="c_pe" if c == NCH - 1 else None)
                    r_wo[j].read(tt)
                    r_ot.read(tt)
                    r_psc[reg].write(tt)

                    def ev(e, reg=reg):
                        return e.activation(
                            at_sb[reg][:, :], ps_c[:, reg * 512:reg * 512 + 512],
                            AF.Copy, bias=0.0, scale=1.0)
                    te = sc.do("scalar", ev,
                               waits=[tt] + r_at[reg].for_write(), inc="c_ac")
                    r_psc[reg].read(te)
                    r_at[reg].write(te)

                    def fst(e, reg=reg, i=i, ob=ob):
                        return e.dma_start(
                            attn_d[i * 128:(i + 1) * 128,
                                   ob * 512:(ob + 1) * 512],
                            at_sb[reg][:, :])
                    ts_ = sc.do("sync", fst, waits=[te], inc=f"c_std{reg}",
                                dma=True)
                    r_at[reg].read(ts_)

            sc.barrier()
            dbg_copy("attn", attn_d)
            sc.barrier()
            sc.flush("phaseC")

    ln3_stack = ExitStack()
    if "C2" in PHASES:
        ln3t = ln3_stack.enter_context(
            nc.sbuf_tensor("d_ln3t", [128, NCH * 2048], BF16))
        r_ln3t = Res()
        with ExitStack() as ph:
            pb = lambda n, s_, d_: ph.enter_context(nc.sbuf_tensor(n, s_, d_))
            x_f = [pb(f"g_x{j}", [128, D], F32) for j in range(2)]
            a_f = [pb(f"g_a{j}", [128, D], F32) for j in range(2)]
            ln_bf = pb("g_ln", [128, D], BF16)
            bn6 = pb("g_bn6", [128, 48], F32)
            mv = pb("g_mv", [128, 2], F32)
            rstd = pb("g_rstd", [128, 1], F32)
            ps_t = [ph.enter_context(nc.psum_tensor(f"g_pst{jj_}", [128, 512], BF16))
                    for jj_ in range(2)]

            r_x = [Res() for _ in range(2)]
            r_a = [Res() for _ in range(2)]
            r_ln = Res()
            r_st = Res()
            r_ps = [Res() for _ in range(2)]

            for i in range(NT):
                j = i % 2
                tx = sc.do("sync", lambda e, j=j, i=i: e.dma_start(
                    x_f[j][:, :], x_in[i * 128:(i + 1) * 128, :]),
                    waits=r_x[j].for_write(), inc=f"g_x{j}", dma=True)
                r_x[j].write(tx)
                ta = sc.do("sync", lambda e, j=j, i=i: e.dma_start(
                    a_f[j][:, :], attn_d[i * 128:(i + 1) * 128, :]),
                    waits=r_a[j].for_write(), inc=f"g_a{j}", dma=True)
                r_a[j].write(ta)

                _, trs = _ln_tile(sc, "g", a_f[j][:, :], bn6, mv, rstd,
                                  eps_sb, [t_eps], r_a[j], r_st)
                tn = sc.do("vector", lambda e, j=j: e.tensor_scalar(
                    a_f[j][:, :], a_f[j][:, :], mv[:, 0:1], rstd[:, 0:1],
                    ALU.subtract, ALU.mult), waits=[trs], inc="g_dv")
                r_a[j].write(tn)
                r_st.read(tn)
                tn2 = sc.do("vector", lambda e, j=j: e.tensor_add(
                    x_f[j][:, :], x_f[j][:, :], a_f[j][:, :]),
                    waits=[tn, tx], inc="g_dv")
                r_x[j].write(tn2)
                r_a[j].read(tn2)
                ts_ = sc.do("sync", lambda e, j=j, i=i: e.dma_start(
                    x2_d[i * 128:(i + 1) * 128, :], x_f[j][:, :]),
                    waits=[tn2], inc=f"g_x2d{j}", dma=True)
                r_x[j].read(ts_)
                _, trs3 = _ln_tile(sc, "g3", x_f[j][:, :], bn6, mv, rstd,
                                   eps_sb, [], r_x[j], r_st)
                tn3 = sc.do("vector", lambda e, j=j: e.tensor_scalar(
                    ln_bf[:, :], x_f[j][:, :], mv[:, 0:1], rstd[:, 0:1],
                    ALU.subtract, ALU.mult),
                    waits=[trs3] + r_ln.for_write(), inc="g_dv")
                r_x[j].read(tn3)
                r_st.read(tn3)
                r_ln.write(tn3)
                for cg in range(8):
                    half = cg % 2
                    tw = [tn3, t_idn] + r_ps[half].for_write()
                    tt = None
                    for cc in range(4):
                        c = cg * 4 + cc
                        tt = sc.do("tensor", lambda e, c=c, half=half, cc=cc:
                                   e.transpose(
                                       ps_t[half][:, cc * 128:(cc + 1) * 128],
                                       ln_bf[:, c * 128:(c + 1) * 128],
                                       idn_sb[:, :]),
                                   waits=tw if cc == 0 else [],
                                   inc="g_pe" if cc == 3 else None)
                    r_ln.read(tt)
                    r_ps[half].write(tt)

                    def evac(e, cg=cg, i=i, half=half):
                        dst = ln3t[:, :].rearrange(
                            "p (c t) -> p c t", c=NCH)[
                            :, cg * 4:(cg + 1) * 4, i * 128:(i + 1) * 128]
                        src = ps_t[half][:, :].rearrange(
                            "p (c t) -> p c t", c=4)[:, :, :]
                        return e.activation(dst, src, AF.Copy,
                                            bias=0.0, scale=1.0)
                    ev = sc.do("scalar", evac, waits=[tt], inc="g_ac")
                    r_ps[half].read(ev)
                    r_ln3t.w.append(ev)

            sc.barrier()
            dbg_copy("x2", x2_d)
            sc.barrier()
            sc.flush("phaseC2")

    if "D" in PHASES:
        with ExitStack() as ph:
            pb = lambda n, s_, d_: ph.enter_context(nc.sbuf_tensor(n, s_, d_))
            rt_sb = pb("d_rt", [96, S], BF16)
            bia_s = pb("d_bias", [1, 512], F32)
            ones1 = pb("d_ones1", [1, 128], F32)
            bb_sb = pb("d_bb", [128, 512], F32)
            bts = [pb(f"d_bt{j}", [96, 512], BF16) for j in range(2)]
            y_sb = [pb(f"d_y{j}", [128, 512], F32) for j in range(2)]
            ps_d = ph.enter_context(nc.psum_tensor("d_ps", [128, 2048], F32))
            ps_s = ph.enter_context(nc.psum_tensor("d_pss", [128, 512], F32))
            ps_r = ph.enter_context(nc.psum_tensor("d_psr", [128, 128], BF16))
            ph1 = ExitStack()
            pb1 = lambda n, s_, d_: ph1.enter_context(nc.sbuf_tensor(n, s_, d_))
            lsi_sb = pb1("d_lsi", [1, 4], F32)
            cb_sb = pb1("d_cb", [1, 1], F32)
            cc_sb = pb1("d_cc", [1, 8], F32)
            ones3 = pb1("d_ones3", [3, 128], F32)
            c3_sb = pb1("d_c3", [3, 1], F32)
            m3 = pb1("d_m3", [3, 1], F32)
            e96_sb = pb1("d_e96", [3, 96], F32)
            me_sb = pb1("d_me", [3, 96], F32)
            m96 = pb1("d_m96", [128, 96], BF16)
            at_sb = pb1("d_at", [128, NCH * 96], BF16)
            rtmp = pb1("d_rtmp", [128, 96], BF16)

            tl1 = sc.do("sync", lambda e: e.dma_start(lsi_sb[:, :], lsi[:, :]),
                        inc="d_cl", dma=True)
            tl2 = sc.do("sync", lambda e: e.dma_start(cb_sb[:, :], cb[:, :]),
                        inc="d_cl", dma=True)
            tl3 = sc.do("sync", lambda e: e.dma_start(c3_sb[:, :], c3[:, :]),
                        inc="d_cl", dma=True)
            tl4 = sc.do("sync", lambda e: e.dma_start(e96_sb[:, :], e96[:, :]),
                        inc="d_cl", dma=True)
            tcl = ("d_cl", sc.cnt["d_cl"])
            tl1 = tl2 = tl3 = tl4 = tcl

            tat = sc.do("gpsimd", lambda e: e.dma_start(
                at_sb[:, :].rearrange("p (c j) -> p c j", c=NCH),
                lat.rearrange("(c p) j -> p c j", p=128)),
                inc="d_atl", dma=True)

            to1 = sc.do("vector", lambda e: e.memset(ones3[:, :], 1.0),
                        inc="d_dv")
            to2 = sc.do("vector", lambda e: e.memset(ones1[:, :], 1.0),
                        inc="d_dv")
            t1 = sc.do("vector", lambda e: e.tensor_single_scalar(
                cc_sb[0:1, 0:4], lsi_sb[0:1, 0:4], cb_sb[0:1, 0:1], ALU.is_gt),
                waits=[tl1, tl2], inc="d_dv")
            t2 = sc.do("vector", lambda e: e.tensor_mul(
                cc_sb[0:1, 4:5], cc_sb[0:1, 0:1], cc_sb[0:1, 1:2]),
                waits=[t1], inc="d_dv")
            t3 = sc.do("vector", lambda e: e.tensor_add(
                cc_sb[0:1, 5:6], cc_sb[0:1, 1:2], cc_sb[0:1, 4:5]),
                waits=[t2], inc="d_dv")
            t4 = sc.do("vector", lambda e: e.tensor_scalar(
                cc_sb[0:1, 5:6], cc_sb[0:1, 5:6], 2.0, -1.0,
                ALU.subtract, ALU.mult), waits=[t3], inc="d_dv")
            t5 = sc.do("vector", lambda e: e.tensor_mul(
                cc_sb[0:1, 5:6], cc_sb[0:1, 5:6], cc_sb[0:1, 2:3]),
                waits=[t4], inc="d_dv")
            t6 = sc.do("vector", lambda e: e.tensor_scalar(
                cc_sb[0:1, 6:7], cc_sb[0:1, 2:3], 1.0, -1.0,
                ALU.subtract, ALU.mult), waits=[t1], inc="d_dv")
            t7 = sc.do("vector", lambda e: e.tensor_scalar(
                cc_sb[0:1, 7:8], cc_sb[0:1, 3:4], 4.0, -1.0,
                ALU.subtract, ALU.mult), waits=[t1], inc="d_dv")
            t8 = sc.do("vector", lambda e: e.tensor_mul(
                cc_sb[0:1, 6:7], cc_sb[0:1, 6:7], cc_sb[0:1, 7:8]),
                waits=[t6, t7], inc="d_dv")
            t9 = sc.do("vector", lambda e: e.tensor_add(
                cc_sb[0:1, 6:7], cc_sb[0:1, 6:7], cc_sb[0:1, 5:6]),
                waits=[t5, t8], inc="d_dv")
            tb1 = sc.do("tensor", lambda e: e.matmul(
                ps_s[0:3, 0:1], ones1[0:1, 0:3], cc_sb[0:1, 6:7],
                start=True, stop=True), waits=[t9, to2], inc="d_pe")
            t10 = sc.do("vector", lambda e: e.tensor_sub(
                m3[0:3, 0:1], ps_s[0:3, 0:1], c3_sb[0:3, 0:1]),
                waits=[tb1, tl3], inc="d_dv")
            t11 = sc.do("scalar", lambda e: e.activation(
                m3[0:3, 0:1], m3[0:3, 0:1], AF.Square, bias=0.0, scale=1.0),
                waits=[t10], inc="d_ac")
            t12 = sc.do("vector", lambda e: e.tensor_scalar(
                m3[0:3, 0:1], m3[0:3, 0:1], 1.0, -1.0,
                ALU.subtract, ALU.mult), waits=[t11], inc="d_dv")
            t13 = sc.do("vector", lambda e: e.tensor_relu(
                m3[0:3, 0:1], m3[0:3, 0:1]), waits=[t12], inc="d_dv")
            t14 = sc.do("vector", lambda e: e.tensor_scalar(
                me_sb[0:3, :], e96_sb[0:3, :], m3[0:3, 0:1], None, ALU.mult),
                waits=[t13, tl4], inc="d_dv")
            tb2 = sc.do("tensor", lambda e: e.matmul(
                ps_s[:, 0:96], ones3[0:3, :], me_sb[0:3, :],
                start=True, stop=True), waits=[t14, to1], inc="d_pe")
            t15 = sc.do("vector", lambda e: e.tensor_copy(
                m96[:, :], ps_s[:, 0:96]), waits=[tb2], inc="d_dv")
            tat_s = []
            for c in range(NCH):
                tt = sc.do("vector", lambda e, c=c: e.tensor_mul(
                    at_sb[:, c * 96:(c + 1) * 96],
                    at_sb[:, c * 96:(c + 1) * 96], m96[:, :]),
                    waits=[t15, tat] if c == 0 else [], inc="d_dv")
                tat_s.append(tt)
            r_rt = Res()
            r_pss = Res()
            r_psr = Res()
            r_psd = [Res() for _ in range(4)]
            for i in range(NT):
                tt = None
                for c in range(NCH):
                    def f(e, c=c, i=i):
                        return e.matmul(
                            ps_s[:, 0:96],
                            ln3t[:, :].rearrange("p (c t) -> p c t", c=NCH)[
                                :, c, i * 128:(i + 1) * 128],
                            at_sb[:, c * 96:(c + 1) * 96],
                            start=(c == 0), stop=(c == NCH - 1))
                    ww = [tat_s[-1], t15, tb2] if i == 0 else []
                    if i > 0:
                        ww = r_pss.for_write()
                    tt = sc.do("tensor", f, waits=ww if c == 0 else [],
                               inc="d_pe" if c == NCH - 1 else None)
                r_pss.write(tt)
                te = sc.do("vector", lambda e: e.tensor_copy(
                    rtmp[:, :], ps_s[:, 0:96]), waits=[tt], inc="d_dv")
                r_pss.read(te)
                ttp = sc.do("tensor", lambda e: e.transpose(
                    ps_r[0:96, :], rtmp[:, :], idn_sb[:, :]),
                    waits=[te, t_idn] + r_psr.for_write(), inc="d_pe")
                r_psr.write(ttp)
                tev = sc.do("vector", lambda e, i=i: e.tensor_copy(
                    rt_sb[0:96, i * 128:(i + 1) * 128], ps_r[0:96, :]),
                    waits=[ttp], inc="d_dv")
                r_psr.read(tev)
                r_rt.w.append(tev)

            sc.barrier()
            if "m96" in dbg:
                tdm = sc.do("sync", lambda e: e.dma_start(
                    dbg["m96"][:, :], m96[:, :]), inc="dbgd", dma=True)
                sc.do("sync", lambda e: None, waits=[tdm])
                if "cc" in dbg:
                    td2 = sc.do("sync", lambda e: e.dma_start(
                        dbg["cc"][:, :], cc_sb[:, :]), inc="dbgd", dma=True)
                    td3 = sc.do("sync", lambda e: e.dma_start(
                        dbg["m3"][:, :], m3[:, :]), inc="dbgd", dma=True)
                    sc.do("sync", lambda e: None, waits=[td3])
                sc.barrier()
            sc.flush("phaseD1")
            ph1.close()
            wt_sb = [pb(f"d_wt{j}", [128, NCH * 512], BF16) for j in range(2)]
            r_wt = [Res() for _ in range(2)]
            r_bt = [Res() for _ in range(2)]
            r_bia = Res()
            r_bb = Res()
            r_y = [Res() for _ in range(2)]

            def load_wt(ob, j, waits):
                tok = None
                for hh in range(2):
                    def f(e, ob=ob, j=j, hh=hh):
                        src = wt.rearrange("(c p) o -> p c o", p=128)[
                            :, hh * 16:(hh + 1) * 16,
                            ob * 512:(ob + 1) * 512]
                        dst = wt_sb[j][:, :].rearrange(
                            "p (c o) -> p c o", c=NCH)[
                            :, hh * 16:(hh + 1) * 16, :]
                        return e.dma_start(dst, src)
                    tok = sc.do("gpsimd", f, waits=waits if hh == 0 else [],
                                inc=f"d_wt{j}", dma=True)
                return tok

            def load_bt(ob, j, waits):
                return sc.do("gpsimd", lambda e, ob=ob, j=j: e.dma_start(
                    bts[j][:, :], lbt[0:96, ob * 512:(ob + 1) * 512]),
                    waits=waits, inc=f"d_bt{j}", dma=True)

            t0 = load_wt(0, 0, [])
            r_wt[0].write(t0)
            tb0 = load_bt(0, 0, [])
            r_bt[0].write(tb0)
            for ob in range(8):
                j = ob % 2
                if ob + 1 < 8:
                    jn = (ob + 1) % 2
                    tn_ = load_wt(ob + 1, jn, r_wt[jn].for_write())
                    r_wt[jn].write(tn_)
                    tnb = load_bt(ob + 1, jn, r_bt[jn].for_write())
                    r_bt[jn].write(tnb)
                tl5 = sc.do("sync", lambda e, ob=ob: e.dma_start(
                    bia_s[:, :], bia[0:1, ob * 512:(ob + 1) * 512]),
                    waits=r_bia.for_write(), inc="d_bl", dma=True)
                r_bia.write(tl5)
                tbb = sc.do("tensor", lambda e: e.matmul(
                    ps_s[:, 0:512], ones1[0:1, :], bia_s[0:1, :],
                    start=True, stop=True),
                    waits=[tl5, to2] + r_pss.for_write(), inc="d_pe")
                r_bia.read(tbb)
                r_pss.write(tbb)
                tbe = sc.do("vector", lambda e: e.tensor_copy(
                    bb_sb[:, :], ps_s[:, 0:512]),
                    waits=[tbb] + r_bb.for_write(), inc="d_dv")
                r_pss.read(tbe)
                r_bb.write(tbe)
                for i in range(NT):
                    reg = (ob * NT + i) % 4
                    w_ = (r_wt[j].for_read() + r_psd[reg].for_write()
                          + (r_rt.w[-1:] if ob == 0 and i == 0 else []))
                    tt = None
                    for c in range(NCH):
                        def f(e, c=c, i=i, j=j, reg=reg):
                            return e.matmul(
                                ps_d[:, reg * 512:reg * 512 + 512],
                                ln3t[:, :].rearrange(
                                    "p (c t) -> p c t", c=NCH)[
                                    :, c, i * 128:(i + 1) * 128],
                                wt_sb[j][:, :].rearrange(
                                    "p (c o) -> p c o", c=NCH)[:, c, :],
                                start=(c == 0), stop=False)
                        tt = sc.do("tensor", f, waits=w_ if c == 0 else [])
                    def flo(e, i=i, j=j, reg=reg):
                        return e.matmul(
                            ps_d[:, reg * 512:reg * 512 + 512],
                            rt_sb[0:96, i * 128:(i + 1) * 128],
                            bts[j][0:96, :],
                            start=False, stop=True)
                    tt = sc.do("tensor", flo,
                               waits=r_bt[j].for_read() if i == 0 else [],
                               inc="d_pe")
                    r_bt[j].read(tt)
                    r_wt[j].read(tt)
                    r_psd[reg].write(tt)

                    yj = i % 2
                    def ev(e, reg=reg, yj=yj):
                        return e.tensor_add(
                            y_sb[yj][:, :], ps_d[:, reg * 512:reg * 512 + 512],
                            bb_sb[:, :])
                    te = sc.do("vector", ev,
                               waits=[tt, tbe] + r_y[yj].for_write(),
                               inc="d_dv")
                    r_psd[reg].read(te)
                    r_bb.read(te)
                    r_y[yj].write(te)

                    def fst(e, yj=yj, i=i, ob=ob):
                        return e.dma_start(
                            y_d[i * 128:(i + 1) * 128,
                                ob * 512:(ob + 1) * 512],
                            y_sb[yj][:, :])
                    ts_ = sc.do("sync", fst, waits=[te], inc=f"d_yd{yj}",
                                dma=True)
                    r_y[yj].read(ts_)

            sc.barrier()
            if "rt" in dbg:
                tdr = sc.do("sync", lambda e: e.dma_start(
                    dbg["rt"][:, :], rt_sb[:, :]), inc="dbgd", dma=True)
                sc.do("sync", lambda e: None, waits=[tdr])
            dbg_copy("y", y_d)
            sc.barrier()
            sc.flush("phaseD")
    ln3_stack.close()

    if "E" in PHASES:
        with ExitStack() as ph:
            pb = lambda n, s_, d_: ph.enter_context(nc.sbuf_tensor(n, s_, d_))
            y_f = [pb(f"e_y{j}", [128, D], F32) for j in range(2)]
            x2_f = [pb(f"e_x{j}", [128, D], F32) for j in range(2)]
            bn6 = pb("e_bn6", [128, 48], F32)
            mv = pb("e_mv", [128, 2], F32)
            rstd = pb("e_rstd", [128, 1], F32)

            r_y = [Res() for _ in range(2)]
            r_x = [Res() for _ in range(2)]
            r_st = Res()

            for i in range(NT):
                j = i % 2
                ty = sc.do("sync", lambda e, j=j, i=i: e.dma_start(
                    y_f[j][:, :], y_d[i * 128:(i + 1) * 128, :]),
                    waits=r_y[j].for_write(), inc=f"e_y{j}", dma=True)
                r_y[j].write(ty)
                tx = sc.do("sync", lambda e, j=j, i=i: e.dma_start(
                    x2_f[j][:, :], x2_d[i * 128:(i + 1) * 128, :]),
                    waits=r_x[j].for_write(), inc=f"e_x{j}", dma=True)
                r_x[j].write(tx)
                _, trs = _ln_tile(sc, "e", y_f[j][:, :], bn6, mv, rstd,
                                  eps_sb, [t_eps], r_y[j], r_st)
                tn = sc.do("vector", lambda e, j=j: e.tensor_scalar(
                    y_f[j][:, :], y_f[j][:, :], mv[:, 0:1], rstd[:, 0:1],
                    ALU.subtract, ALU.mult), waits=[trs], inc="e_dv")
                r_y[j].write(tn)
                r_st.read(tn)
                tn2 = sc.do("vector", lambda e, j=j: e.tensor_add(
                    x2_f[j][:, :], x2_f[j][:, :], y_f[j][:, :]),
                    waits=[tn, tx], inc="e_dv")
                r_x[j].write(tn2)
                r_y[j].read(tn2)
                ts_ = sc.do("sync", lambda e, j=j, i=i: e.dma_start(
                    out[i * 128:(i + 1) * 128, :], x2_f[j][:, :]),
                    waits=[tn2], inc=f"e_out{j}", dma=True)
                r_x[j].read(ts_)
            sc.do("sync", lambda e: None,
                  waits=[("e_out0", sc.cnt["e_out0"]),
                         ("e_out1", sc.cnt["e_out1"])])
            sc.flush("phaseE")

    gx.close()
    return nc


def kernel(
    L0_attn_hidden_states,
    L0_attn_Wq,
    L0_attn_Wk,
    L0_attn_Wv,
    L0_attn_Wo,
    L0_ffn_weight,
    L0_ffn_bias,
    L0_ffn_lora_A,
    L0_ffn_lora_B,
    L0_ffn_lora_start_indices,
):
    consts = _host_consts()
    x = np.ascontiguousarray(np.asarray(L0_attn_hidden_states, dtype=np.float32))
    wq_ = np.ascontiguousarray(np.asarray(L0_attn_Wq, dtype=np.float32))
    wk_ = np.ascontiguousarray(np.asarray(L0_attn_Wk, dtype=np.float32))
    wv_ = np.ascontiguousarray(np.asarray(L0_attn_Wv, dtype=np.float32))
    wo_ = np.ascontiguousarray(np.asarray(L0_attn_Wo, dtype=np.float32))
    wt_ = np.ascontiguousarray(np.asarray(L0_ffn_weight, dtype=np.float32).T)
    bia_ = np.ascontiguousarray(
        np.asarray(L0_ffn_bias, dtype=np.float32))[None, :]
    la = np.asarray(L0_ffn_lora_A, dtype=np.float32)
    lb = np.asarray(L0_ffn_lora_B, dtype=np.float32)
    lat_ = np.zeros((D, 96), dtype=np.float32)
    lbt_ = np.zeros((96, D), dtype=np.float32)
    for n in range(3):
        lat_[:, n * 32: n * 32 + 16] = la[n].T
        lbt_[n * 32: n * 32 + 16, :] = lb[n].T
    lsi_ = np.ascontiguousarray(
        np.asarray(L0_ffn_lora_start_indices).astype(np.float32))[None, :]

    nc = build_nc()
    in_maps = []
    for b in range(8):
        in_maps.append({
            "x": x[b], "wq": wq_, "wk": wk_, "wv": wv_, "wo": wo_, "wt": wt_,
            "bia": bia_, "lat": lat_, "lbt": lbt_, "lsi": lsi_,
            "cb": np.array([[b]], dtype=np.float32),
            "csq": consts["cs_q"], "csk": consts["cs_k"],
            "tri": consts["tri"], "idn": consts["idn"],
            "e96": consts["e96"], "c3": consts["c3"],
        })
    res = run_bass_kernel_spmd(nc, in_maps, core_ids=list(range(8)))
    return np.stack([np.asarray(r["out"], dtype=np.float32) for r in res.results])
